# version 04
# baseline (speedup 1.0000x reference)
import contextlib
import math
import numpy as np
import concourse.bass as bass
import concourse.mybir as mybir
from concourse.bass_utils import run_bass_kernel_spmd

F32 = mybir.dt.float32
BF16 = mybir.dt.bfloat16
I32 = mybir.dt.int32
AF = mybir.ActivationFunctionType
ALU = mybir.AluOpType

NB, NO, FO, D = 4096, 2048, 2048, 4096
SCALE = 1.0 / math.sqrt(128.0)
EPS = 1e-5


def build(debug_stage=None):
    nc = bass.Bass("TRN2", target_bir_lowering=False)
    es = contextlib.ExitStack()

    xb = nc.dram_tensor("xb", [NB, D], F32, kind="ExternalInput").ap()
    x_own = nc.dram_tensor("x_own", [NO, D], F32, kind="ExternalInput").ap()
    wq_in = nc.dram_tensor("wq", [FO, D], F32, kind="ExternalInput").ap()
    wk_in = nc.dram_tensor("wk", [FO, D], F32, kind="ExternalInput").ap()
    wv_in = nc.dram_tensor("wv", [FO, D], F32, kind="ExternalInput").ap()
    wo_in = nc.dram_tensor("wo", [D, FO], F32, kind="ExternalInput").ap()
    bq_in = nc.dram_tensor("bq", [16, 128], F32, kind="ExternalInput").ap()
    bk_in = nc.dram_tensor("bk", [16, 128], F32, kind="ExternalInput").ap()
    bv_in = nc.dram_tensor("bv", [16, 128], F32, kind="ExternalInput").ap()
    bo_in = nc.dram_tensor("bo", [1, D], F32, kind="ExternalInput").ap()
    bp_in = nc.dram_tensor("bp", [D, D // 2], I32, kind="ExternalInput").ap()
    sc_in = nc.dram_tensor("sc", [32, 128], F32, kind="ExternalInput").ap()
    wort_in = nc.dram_tensor("wort", [D, D], F32, kind="ExternalInput").ap()
    out = nc.dram_tensor("out", [NO, D], F32, kind="ExternalOutput").ap()

    dbg = {"p2": ["s_qT", "s_kT", "s_vT"], "p3": ["s_attnT"], "p5a": ["s_x1"]}.get(debug_stage, [])

    def scratch(name, shape, dtype):
        kind = "ExternalOutput" if name in dbg else "Internal"
        return nc.dram_tensor(name, shape, dtype, kind=kind).ap()

    s_wqT = scratch("s_wqT", [16 * 128, 32 * 128], BF16)
    s_wkT = scratch("s_wkT", [16 * 128, 32 * 128], BF16)
    s_wvT = scratch("s_wvT", [16 * 128, 32 * 128], BF16)
    s_woT = scratch("s_woT", [FO, D], BF16)
    s_w = scratch("s_w", [8 * 128, 32 * 512], BF16)
    s_qT = scratch("s_qT", [FO, NB], BF16)
    s_kT = scratch("s_kT", [FO, NB], BF16)
    s_vT = scratch("s_vT", [FO, NB], BF16)
    s_attnT = scratch("s_attnT", [FO, NB], BF16)
    s_part = nc.dram_tensor("s_part", [NB, D], BF16).ap()
    s_prs = nc.dram_tensor("s_prs", [NO, D], BF16).ap()
    s_x1 = scratch("s_x1", [NO, D], F32)

    blk = es.enter_context(nc.Block())

    class Sem:
        def __init__(self, name):
            self.h = es.enter_context(nc.semaphore(name))
            self.n = 0
        def inc(self, k=1):
            self.n += k
            return self.n

    sy = Sem("sy"); sa = Sem("sa"); sg = Sem("sg")
    st = Sem("st"); sv = Sem("sv"); ss = Sem("ss")
    scc = Sem("scc"); so = Sem("so")
    ALLS = [sy, sa, sg, st, sv, ss]

    def T(fn): blk.tensor(fn)
    def V(fn): blk.vector(fn)
    def S(fn): blk.scalar(fn)
    def G(fn): blk.gpsimd(fn)
    def Y(fn): blk.sync(fn)

    def fence():
        marks = [(s.h, s.n) for s in ALLS if s.n > 0]
        for E in (T, V, S, G, Y):
            def f(e, marks=marks):
                for h, v in marks:
                    e.wait_ge(h, v)
            E(f)

    ident = es.enter_context(nc.sbuf_tensor("ident", [128, 128], BF16))
    io_q = es.enter_context(nc.sbuf_tensor("io_q", [128, 512], F32))
    io_col = es.enter_context(nc.sbuf_tensor("io_col", [128, 1], F32))
    masks = es.enter_context(nc.sbuf_tensor("masks", [128, 4 * 512], BF16))
    thr = es.enter_context(nc.sbuf_tensor("thr", [128, 4], F32))
    ones_col = es.enter_context(nc.sbuf_tensor("ones_col", [128, 1], BF16))
    ones_row = es.enter_context(nc.sbuf_tensor("ones_row", [1, 128], BF16))
    bq_sb = es.enter_context(nc.sbuf_tensor("bq_sb", [128, 16], F32))
    bk_sb = es.enter_context(nc.sbuf_tensor("bk_sb", [128, 16], F32))
    bv_sb = es.enter_context(nc.sbuf_tensor("bv_sb", [128, 16], F32))
    bo_bf = es.enter_context(nc.sbuf_tensor("bo_bf", [1, D], BF16))
    sc_sb = es.enter_context(nc.sbuf_tensor("sc_sb", [128, 32], F32))
    m8s_sb = es.enter_context(nc.sbuf_tensor("m8s_sb", [128, 32], F32))
    junk1 = es.enter_context(nc.sbuf_tensor("junk1", [128, D], BF16))
    junk2 = es.enter_context(nc.sbuf_tensor("junk2", [128, D], BF16))
    st_sum = es.enter_context(nc.sbuf_tensor("st_sum", [128, 1], F32))
    st_sq = es.enter_context(nc.sbuf_tensor("st_sq", [128, 1], F32))
    st_mean = es.enter_context(nc.sbuf_tensor("st_mean", [128, 1], F32))
    st_var = es.enter_context(nc.sbuf_tensor("st_var", [128, 1], F32))
    st_std = es.enter_context(nc.sbuf_tensor("st_std", [128, 1], F32))
    st_rstd = es.enter_context(nc.sbuf_tensor("st_rstd", [128, 1], F32))
    st_nm = es.enter_context(nc.sbuf_tensor("st_nm", [128, 1], F32))

    tp_state = {"g": 0, "marks": {}, "banks": None}

    def emit_transposes(src_fn, nch, evac_dst_fn, t_waits=(), v_waits=()):
        ps_tp = tp_state["banks"]
        groups = [(g0, min(8, nch - g0)) for g0 in range(0, nch, 8)]
        for gi, (g0, glen) in enumerate(groups):
            g = tp_state["g"]
            bank = ps_tp[g % 2]
            prev = tp_state["marks"].get(g - 2)
            def tfn(t, gi=gi, g0=g0, glen=glen, bank=bank, prev=prev):
                if gi == 0:
                    for (h, v) in t_waits:
                        t.wait_ge(h, v)
                if prev is not None:
                    t.wait_ge(sv.h, prev)
                for k in range(glen):
                    ins = t.transpose(bank[:, k * 128:(k + 1) * 128], src_fn(g0 + k), ident[:])
                    if k == glen - 1:
                        ins.then_inc(st.h, 1)
            T(tfn)
            st.inc()
            tmark = st.n
            def vfn(v, gi=gi, g0=g0, glen=glen, bank=bank, tmark=tmark):
                if gi == 0:
                    for (h, vv) in v_waits:
                        v.wait_ge(h, vv)
                v.wait_ge(st.h, tmark)
                v.tensor_copy(out=evac_dst_fn(g0, glen), in_=bank[:, 0:glen * 128]).then_inc(sv.h, 1)
            V(vfn)
            sv.inc()
            tp_state["marks"][g] = sv.n
            tp_state["g"] += 1
        return st.n, sv.n

    def emit_ln(x_ap, normalize, n_free, s_waits=(), v_waits=()):
        inv_n = 1.0 / n_free
        def sfn(s):
            for (h, v) in s_waits:
                s.wait_ge(h, v)
            s.activation(out=junk1[:, 0:n_free], in_=x_ap, func=AF.Square,
                         accum_out=st_sq[:]).then_inc(ss.h, 1)
        S(sfn)
        ss.inc()
        sqm = ss.n
        sv0 = sv.n
        def vfn(v, sv0=sv0):
            for (h, vv) in v_waits:
                v.wait_ge(h, vv)
            v.tensor_scalar(out=junk2[:, 0:n_free], in0=x_ap, scalar1=1.0, scalar2=0.0,
                            op0=ALU.mult, op1=ALU.add, accum_out=st_sum[:]).then_inc(sv.h, 1)
            v.wait_ge(sv.h, sv0 + 1)
            v.wait_ge(ss.h, sqm)
            v.tensor_scalar(out=st_mean[:], in0=st_sum[:], scalar1=inv_n, scalar2=None, op0=ALU.mult)
            v.tensor_scalar(out=st_var[:], in0=st_sq[:], scalar1=inv_n, scalar2=EPS,
                            op0=ALU.mult, op1=ALU.add)
            v.scalar_tensor_tensor(out=st_nm[:], in0=st_mean[:], scalar=-1.0, in1=st_mean[:],
                                   op0=ALU.mult, op1=ALU.mult).then_inc(sv.h, 1)
            v.wait_ge(sv.h, sv0 + 2)
            v.tensor_tensor(out=st_var[:], in0=st_var[:], in1=st_nm[:], op=ALU.add).then_inc(sv.h, 1)
        V(vfn)
        sv.inc(3)
        vm1 = sv.n
        def sfn2(s):
            s.wait_ge(sv.h, vm1)
            s.activation(out=st_std[:], in_=st_var[:], func=AF.Sqrt).then_inc(ss.h, 1)
        S(sfn2)
        ss.inc()
        sq2 = ss.n
        sv1 = sv.n
        def vfn2(v, sv1=sv1):
            v.wait_ge(ss.h, sq2)
            v.reciprocal(out=st_rstd[:], in_=st_std[:]).then_inc(sv.h, 1)
            v.wait_ge(sv.h, sv1 + 1)
            v.scalar_tensor_tensor(out=st_nm[:], in0=st_mean[:], scalar=-1.0, in1=st_rstd[:],
                                   op0=ALU.mult, op1=ALU.mult).then_inc(sv.h, 1)
        V(vfn2)
        sv.inc(2)
        vm2 = sv.n
        def sfn3(s):
            s.wait_ge(sv.h, vm2)
            normalize(s, st_rstd[:], st_nm[:])
        S(sfn3)
        return vm2

    def g0(g):
        g.iota(io_q[:], [[1, 512]], base=0, channel_multiplier=0,
               allow_small_or_imprecise_dtypes=True)
        g.iota(io_col[:], [[0, 1]], base=0, channel_multiplier=1,
               allow_small_or_imprecise_dtypes=True)
        for d in range(4):
            g.iota(thr[:, d:d + 1], [[0, 1]], base=128 * d, channel_multiplier=1,
                   allow_small_or_imprecise_dtypes=True)
        g.memset(ones_col[:], 1.0)
        g.memset(ones_row[:], 1.0)
        for j in range(16):
            g.dma_start(out=bq_sb[:, j:j + 1], in_=bq_in[j].rearrange("(p one) -> p one", one=1)).then_inc(sg.h, 16)
            g.dma_start(out=bk_sb[:, j:j + 1], in_=bk_in[j].rearrange("(p one) -> p one", one=1)).then_inc(sg.h, 16)
            g.dma_start(out=bv_sb[:, j:j + 1], in_=bv_in[j].rearrange("(p one) -> p one", one=1)).then_inc(sg.h, 16)
        for j in range(32):
            g.dma_start(out=sc_sb[:, j:j + 1], in_=sc_in[j].rearrange("(p one) -> p one", one=1)).then_inc(sg.h, 16)
        g.dma_start(out=bo_bf[:], in_=bo_in).then_inc(sg.h, 16)
    G(g0)
    sg.inc(16 * 81)
    start_g = sg.n
    def v0(v):
        v.wait_ge(sg.h, start_g)
        v.tensor_scalar(out=ident[:], in0=io_q[:, 0:128], scalar1=io_col[:], scalar2=None, op0=ALU.is_equal)
        for d in range(4):
            v.tensor_scalar(out=masks[:, d * 512:(d + 1) * 512], in0=io_q[:],
                            scalar1=thr[:, d:d + 1], scalar2=None, op0=ALU.is_ge)
        v.tensor_scalar(out=m8s_sb[:], in0=sc_sb[:], scalar1=-8.0, scalar2=None, op0=ALU.mult).then_inc(sv.h, 1)
    V(v0)
    sv.inc()
    startup_done = sv.n

    p0 = contextlib.ExitStack()
    tp_state["banks"] = [p0.enter_context(nc.psum_tensor(f"ps_tp0{i}", [128, 1024], BF16)) for i in range(2)]
    w_in = [p0.enter_context(nc.sbuf_tensor(f"w_in{i}", [128, D], F32)) for i in range(2)]
    w_bf = [p0.enter_context(nc.sbuf_tensor(f"w_bf{i}", [128, D], BF16)) for i in range(2)]
    stg = [p0.enter_context(nc.sbuf_tensor(f"stg{i}", [128, D], BF16)) for i in range(2)]
    bp_b = [p0.enter_context(nc.sbuf_tensor(f"bp_b{i}", [128, D // 2], I32)) for i in range(2)]
    ort_b = [p0.enter_context(nc.sbuf_tensor(f"ort_b{i}", [128, D], F32)) for i in range(2)]
    lo_sb = p0.enter_context(nc.sbuf_tensor("lo_sb", [128, D // 2], I32))
    hi_sb = p0.enter_context(nc.sbuf_tensor("hi_sb", [128, D // 2], I32))
    dq_e = p0.enter_context(nc.sbuf_tensor("dq_e", [128, D // 2], F32))
    dq_o = p0.enter_context(nc.sbuf_tensor("dq_o", [128, D // 2], F32))

    def p0_std(w_ap, dst):
        conv, tpm, wr = {}, {}, {}
        for j in range(16):
            binp, bbf, bstg = w_in[j % 2], w_bf[j % 2], stg[j % 2]
            def yfn(y, j=j, binp=binp):
                if conv.get(j - 2):
                    y.wait_ge(ss.h, conv[j - 2])
                y.dma_start(out=binp[:], in_=w_ap[j * 128:(j + 1) * 128, :]).then_inc(sy.h, 16)
            Y(yfn)
            sy.inc(16); ym = sy.n
            def sfn(s, j=j, binp=binp, bbf=bbf, ym=ym):
                s.wait_ge(sy.h, ym)
                if tpm.get(j - 2):
                    s.wait_ge(st.h, tpm[j - 2])
                s.activation(out=bbf[:], in_=binp[:], func=AF.Identity).then_inc(ss.h, 1)
            S(sfn)
            ss.inc(); conv[j] = ss.n; cm = ss.n
            twl = [(ss.h, cm)] + ([(sv.h, startup_done)] if j == 0 else [])
            vwl = [(sg.h, wr[j - 2])] if wr.get(j - 2) else []
            tm, vm = emit_transposes(
                src_fn=lambda c, bbf=bbf: bbf[:, c * 128:(c + 1) * 128], nch=32,
                evac_dst_fn=lambda g0, glen, bstg=bstg: bstg[:, g0 * 128:(g0 + glen) * 128],
                t_waits=twl, v_waits=vwl)
            tpm[j] = tm
            def gfn(g, j=j, bstg=bstg, vm=vm, dst=dst):
                g.wait_ge(sv.h, vm)
                g.dma_start(out=dst[j * 128:(j + 1) * 128, :], in_=bstg[:]).then_inc(sg.h, 16)
            G(gfn)
            sg.inc(16); wr[j] = sg.n

    def p0_wo():
        conv, tpm, wr = {}, {}, {}
        for j in range(16):
            binp, bbf, bstg = w_in[j % 2], w_bf[j % 2], stg[j % 2]
            def yfn(y, j=j, binp=binp):
                if conv.get(j - 2):
                    y.wait_ge(ss.h, conv[j - 2])
                for r in range(32):
                    y.dma_start(out=binp[:, r * 128:(r + 1) * 128],
                                in_=wo_in[r * 128:(r + 1) * 128, j * 128:(j + 1) * 128]
                                ).then_inc(sy.h, 16)
            Y(yfn)
            sy.inc(16 * 32); ym = sy.n
            def sfn(s, j=j, binp=binp, bbf=bbf, ym=ym):
                s.wait_ge(sy.h, ym)
                if tpm.get(j - 2):
                    s.wait_ge(st.h, tpm[j - 2])
                s.activation(out=bbf[:], in_=binp[:], func=AF.Identity).then_inc(ss.h, 1)
            S(sfn)
            ss.inc(); conv[j] = ss.n; cm = ss.n
            vwl = [(sg.h, wr[j - 2])] if wr.get(j - 2) else []
            tm, vm = emit_transposes(
                src_fn=lambda c, bbf=bbf: bbf[:, c * 128:(c + 1) * 128], nch=32,
                evac_dst_fn=lambda g0, glen, bstg=bstg: bstg[:, g0 * 128:(g0 + glen) * 128],
                t_waits=[(ss.h, cm)], v_waits=vwl)
            tpm[j] = tm
            def gfn(g, j=j, bstg=bstg, vm=vm):
                g.wait_ge(sv.h, vm)
                g.dma_start(out=s_woT[j * 128:(j + 1) * 128, :], in_=bstg[:]).then_inc(sg.h, 16)
            G(gfn)
            sg.inc(16); wr[j] = sg.n

    def p0_ffn():
        conv, tpm, wr = {}, {}, {}
        for r in range(32):
            bbp, bor, bbf, bstg = bp_b[r % 2], ort_b[r % 2], w_bf[r % 2], stg[r % 2]
            def yfn(y, r=r, bbp=bbp, bor=bor):
                if conv.get(r - 2):
                    y.wait_ge(sv.h, conv[r - 2])
                y.dma_start(out=bbp[:], in_=bp_in[r * 128:(r + 1) * 128, :]).then_inc(sy.h, 16)
                y.dma_start(out=bor[:], in_=wort_in[r * 128:(r + 1) * 128, :]).then_inc(sy.h, 16)
            Y(yfn)
            sy.inc(32); ym = sy.n
            def vfn(v, r=r, bbp=bbp, bor=bor, bbf=bbf, ym=ym):
                v.wait_ge(sy.h, ym)
                if tpm.get(r - 2):
                    v.wait_ge(st.h, tpm[r - 2])
                v.tensor_scalar(out=lo_sb[:], in0=bbp[:], scalar1=15, scalar2=None, op0=ALU.bitwise_and)
                v.tensor_scalar(out=hi_sb[:], in0=bbp[:], scalar1=4, scalar2=None, op0=ALU.arith_shift_right)
                v.tensor_scalar(out=dq_e[:], in0=lo_sb[:], scalar1=sc_sb[:, r:r + 1],
                                scalar2=m8s_sb[:, r:r + 1], op0=ALU.mult, op1=ALU.add)
                v.tensor_scalar(out=dq_o[:], in0=hi_sb[:], scalar1=sc_sb[:, r:r + 1],
                                scalar2=m8s_sb[:, r:r + 1], op0=ALU.mult, op1=ALU.add)
                v.tensor_tensor(out=bbf[:, 0:D:2], in0=dq_e[:], in1=bor[:, 0:D:2], op=ALU.add)
                v.tensor_tensor(out=bbf[:, 1:D:2], in0=dq_o[:], in1=bor[:, 1:D:2], op=ALU.add).then_inc(sv.h, 1)
            V(vfn)
            sv.inc(); conv[r] = sv.n; cm = sv.n
            vwl = [(sg.h, wr[r - 2])] if wr.get(r - 2) else []
            tm, vm = emit_transposes(
                src_fn=lambda c, bbf=bbf: bbf[:, c * 128:(c + 1) * 128], nch=32,
                evac_dst_fn=lambda g0, glen, bstg=bstg: bstg[:, g0 * 128:(g0 + glen) * 128],
                t_waits=[(sv.h, cm)], v_waits=vwl)
            tpm[r] = tm
            ot, rr = r // 4, r % 4
            def gfn(g, r=r, bstg=bstg, vm=vm, ot=ot, rr=rr):
                g.wait_ge(sv.h, vm)
                dstblk = s_w[ot * 128:(ot + 1) * 128, :].rearrange("p (c o) -> p c o", o=512)
                dst = dstblk[:, :, rr * 128:(rr + 1) * 128]
                src = bstg[:].rearrange("p (c o) -> p c o", o=128)
                for h4 in range(4):
                    g.dma_start(out=dst[:, h4 * 8:(h4 + 1) * 8, :],
                                in_=src[:, h4 * 8:(h4 + 1) * 8, :],
                                allow_slow_non_contiguous=True).then_inc(sg.h, 16)
            G(gfn)
            sg.inc(64); wr[r] = sg.n

    p0_std(wq_in, s_wqT)
    p0_std(wk_in, s_wkT)
    p0_std(wv_in, s_wvT)
    p0_wo()
    p0_ffn()
    p0.close()
    fence()

    p2 = contextlib.ExitStack()
    tp_state["banks"] = [p2.enter_context(nc.psum_tensor(f"ps_tp2{i}", [128, 1024], BF16)) for i in range(2)]
    x_b = [p2.enter_context(nc.sbuf_tensor(f"x_b{i}", [128, D], F32)) for i in range(2)]
    xn_tok = [p2.enter_context(nc.sbuf_tensor(f"xn_tok{i}", [128, D], BF16)) for i in range(2)]
    xnT = [p2.enter_context(nc.sbuf_tensor(f"xnT{i}", [128, 32 * 512], BF16)) for i in range(2)]
    wblk = [p2.enter_context(nc.sbuf_tensor(f"wblk{i}", [128, D], BF16)) for i in range(2)]
    pstg = [p2.enter_context(nc.sbuf_tensor(f"pstg{i}", [128, 512], BF16)) for i in range(2)]
    ps_mm = [p2.enter_context(nc.psum_tensor(f"ps_mm{i}", [128, 512], F32)) for i in range(2)]

    WLIST = [("q", s_wqT, s_qT, bq_sb), ("k", s_wkT, s_kT, bk_sb), ("v", s_wvT, s_vT, bv_sb)]
    xn_used = {}
    xn_tok_tp = {}
    wblk_used = {}
    evac_marks = {}
    pstg_w = {}
    xload = {}
    widx = 0

    for e in range(8):
        nbuf = xnT[e % 2]
        for tci in range(4):
            tc = e * 4 + tci
            xbuf = x_b[tci % 2]
            ntk = xn_tok[tci % 2]
            gtc = e * 4 + tci
            def yfn(y, e=e, tci=tci, tc=tc, xbuf=xbuf):
                pm = xload.get((e, tci - 2)) or (xload.get((e - 1, tci + 2)) if tci < 2 else None)
                if pm:
                    y.wait_ge(ss.h, pm)
                y.dma_start(out=xbuf[:], in_=xb[tc * 128:(tc + 1) * 128, :]).then_inc(sy.h, 16)
            Y(yfn)
            sy.inc(16); ym = sy.n
            ntk_free = xn_tok_tp.get(gtc - 2)
            def normalize(s, scl, bias, xbuf=xbuf, ntk=ntk, ntk_free=ntk_free):
                if ntk_free:
                    s.wait_ge(st.h, ntk_free)
                s.activation(out=ntk[:], in_=xbuf[:], func=AF.Identity,
                             scale=scl, bias=bias).then_inc(ss.h, 1)
            emit_ln(xbuf[:], normalize, D, s_waits=[(sy.h, ym)], v_waits=[(sy.h, ym)])
            ss.inc()
            xload[(e, tci)] = ss.n
            nm = ss.n
            vwl = [(st.h, xn_used[e - 2])] if xn_used.get(e - 2) else []
            tm, _ = emit_transposes(
                src_fn=lambda c, ntk=ntk: ntk[:, c * 128:(c + 1) * 128], nch=32,
                evac_dst_fn=lambda g0, glen, nbuf=nbuf, tci=tci:
                    nbuf[:].rearrange("p (c n) -> p c n", n=512)[:, g0:g0 + glen, tci * 128:(tci + 1) * 128],
                t_waits=[(ss.h, nm)], v_waits=vwl)
            xn_tok_tp[gtc] = tm
        xnT_ready = sv.n

        for (wname, wscr, dscr, bias_sb) in WLIST:
            for j in range(16):
                wb = wblk[widx % 2]
                def yfn(y, wb=wb, wscr=wscr, j=j, widx=widx):
                    um = wblk_used.get(widx - 2)
                    if um:
                        y.wait_ge(st.h, um)
                    y.dma_start(out=wb[:], in_=wscr[j * 128:(j + 1) * 128, :]).then_inc(sy.h, 16)
                Y(yfn)
                sy.inc(16); wm = sy.n
                bank = ps_mm[widx % 2]
                def tfn(t, wb=wb, bank=bank, wm=wm, widx=widx, nbuf=nbuf, xnT_ready=xnT_ready):
                    t.wait_ge(sy.h, wm)
                    t.wait_ge(sv.h, xnT_ready)
                    um = evac_marks.get(widx - 2)
                    if um:
                        t.wait_ge(ss.h, um)
                    rhs = nbuf[:].rearrange("p (c n) -> p c n", n=512)
                    ins = None
                    for c in range(32):
                        ins = t.matmul(bank[:], wb[:, c * 128:(c + 1) * 128], rhs[:, c, :],
                                       start=(c == 0), stop=(c == 31))
                    ins.then_inc(st.h, 1)
                T(tfn)
                st.inc(); mm = st.n
                wblk_used[widx] = mm
                pbuf = pstg[widx % 2]
                def sfn(s, bank=bank, pbuf=pbuf, mm=mm, bias_sb=bias_sb, j=j, widx=widx):
                    s.wait_ge(st.h, mm)
                    um = pstg_w.get(widx - 2)
                    if um:
                        s.wait_ge(sa.h, um)
                    s.activation(out=pbuf[:], in_=bank[:], func=AF.Identity,
                                 bias=bias_sb[:, j:j + 1], scale=1.0).then_inc(ss.h, 1)
                S(sfn)
                ss.inc(); em = ss.n
                evac_marks[widx] = em
                def afn(a, pbuf=pbuf, dscr=dscr, j=j, e=e, em=em):
                    a.wait_ge(ss.h, em)
                    a.dma_start(out=dscr[j * 128:(j + 1) * 128, e * 512:(e + 1) * 512],
                                in_=pbuf[:]).then_inc(sa.h, 16)
                S(afn)
                sa.inc(16)
                pstg_w[widx] = sa.n
                widx += 1
        xn_used[e] = st.n

    p2.close()
    fence()
    if debug_stage == "p2":
        es.close()
        return nc

    p3 = contextlib.ExitStack()
    qh = [p3.enter_context(nc.sbuf_tensor(f"qh{i}", [128, NB], BF16)) for i in range(2)]
    kh = [p3.enter_context(nc.sbuf_tensor(f"kh{i}", [128, NB], BF16)) for i in range(2)]
    vh = [p3.enter_context(nc.sbuf_tensor(f"vh{i}", [128, NB], BF16)) for i in range(2)]
    vtok = p3.enter_context(nc.sbuf_tensor("vtok", [128, NB], BF16))
    p_sb = [p3.enter_context(nc.sbuf_tensor(f"p_sb{i}", [128, 512], BF16)) for i in range(4)]
    rbf = [p3.enter_context(nc.sbuf_tensor(f"rbf{i}", [1, 512], F32)) for i in range(2)]
    rbb = [p3.enter_context(nc.sbuf_tensor(f"rbb{i}", [1, 512], BF16)) for i in range(2)]
    rbc = p3.enter_context(nc.sbuf_tensor("rbc", [128, 512], F32))
    astg = [p3.enter_context(nc.sbuf_tensor(f"astg{i}", [128, 512], BF16)) for i in range(2)]
    ps_s = [p3.enter_context(nc.psum_tensor(f"ps_s{i}", [128, 512], F32)) for i in range(3)]
    ps_o = [p3.enter_context(nc.psum_tensor(f"ps_o{i}", [128, 512], F32)) for i in range(2)]
    ps_u = [p3.enter_context(nc.psum_tensor(f"ps_u{i}", [128, 512], F32)) for i in range(2)]

    head_done = {}
    smark = {}
    uomark = {}
    expmark = {}
    maskmark = {}
    v1mark = {}
    rank1m = {}
    rcopy = {}
    evacA = {}
    astg_w = {}
    gS = [0]
    expglobal = {}
    gidx = {}

    def e_S(hl, qt, kc, extra=()):
        g = gS[0]
        bank = g % 3
        pgu = expglobal.get(g - 3)
        def f(t, hl=hl, qt=qt, kc=kc, bank=bank, pgu=pgu, extra=tuple(extra)):
            for (h, v) in extra:
                t.wait_ge(h, v)
            if pgu:
                t.wait_ge(ss.h, expmark[pgu])
            t.matmul(ps_s[bank][:], kh[hl % 2][:, kc * 128:(kc + 1) * 128],
                     qh[hl % 2][:, qt * 512:(qt + 1) * 512],
                     start=True, stop=True).then_inc(st.h, 1)
        T(f)
        st.inc()
        smark[(hl, qt, kc)] = st.n
        expglobal[g] = (hl, qt, kc)
        gidx[(hl, qt, kc)] = g
        gS[0] += 1
        return g

    def e_EXP(hl, qt, kc, g):
        bank = g % 3
        buf = g % 4
        pb = expglobal.get(g - 4)
        def f(s, hl=hl, qt=qt, kc=kc, bank=bank, buf=buf, pb=pb):
            s.wait_ge(st.h, smark[(hl, qt, kc)])
            if pb is not None:
                s.wait_ge(st.h, uomark[pb])
            s.activation(out=p_sb[buf][:], in_=ps_s[bank][:], func=AF.Exp,
                         scale=SCALE).then_inc(ss.h, 1)
        S(f)
        ss.inc()
        expmark[(hl, qt, kc)] = ss.n

    def e_MASK(hl, qt, kc, g):
        buf = g % 4
        dd = kc - 4 * qt
        def f(v, hl=hl, qt=qt, kc=kc, buf=buf, dd=dd):
            v.wait_ge(ss.h, expmark[(hl, qt, kc)])
            v.tensor_tensor(out=p_sb[buf][:], in0=p_sb[buf][:],
                            in1=masks[:, dd * 512:(dd + 1) * 512], op=ALU.mult).then_inc(sv.h, 1)
        V(f)
        sv.inc()
        maskmark[(hl, qt, kc)] = sv.n

    def e_UO(hl, qt, kc, K, vt_ready):
        buf = gidx[(hl, qt, kc)] % 4
        sb_ = qt % 2
        diag = kc >= 4 * qt
        first, last = kc == 0, kc == K - 1
        prev_same = (hl, qt - 2) if qt >= 2 else ((hl - 1, qt + 6) if hl >= 1 else None)
        def f(t, hl=hl, qt=qt, kc=kc, buf=buf, sb_=sb_, diag=diag, first=first,
              last=last, prev_same=prev_same, vt_ready=vt_ready):
            if diag:
                t.wait_ge(sv.h, maskmark[(hl, qt, kc)])
            else:
                t.wait_ge(ss.h, expmark[(hl, qt, kc)])
            if first:
                t.wait_ge(sv.h, vt_ready)
                if prev_same:
                    if prev_same in rcopy:
                        t.wait_ge(sv.h, rcopy[prev_same])
                    if prev_same in evacA:
                        t.wait_ge(sv.h, evacA[prev_same])
            t.matmul(ps_u[sb_][0:1, :], ones_col[:], p_sb[buf][:],
                     start=first, stop=last)
            t.matmul(ps_o[sb_][:], vtok[:, kc * 128:(kc + 1) * 128], p_sb[buf][:],
                     start=first, stop=last).then_inc(st.h, 1)
        T(f)
        st.inc()
        uomark[(hl, qt, kc)] = st.n

    def e_V1(hl, qt, K):
        sb_ = qt % 2
        prev_rb = (hl, qt - 2) if qt >= 2 else ((hl - 1, qt + 6) if hl >= 1 else None)
        def f(v, hl=hl, qt=qt, K=K, sb_=sb_, prev_rb=prev_rb):
            v.wait_ge(st.h, uomark[(hl, qt, K - 1)])
            if prev_rb and prev_rb in rank1m:
                v.wait_ge(st.h, rank1m[prev_rb])
            v.tensor_copy(out=rbf[sb_][:], in_=ps_u[sb_][0:1, :])
            v.reciprocal(out=rbf[sb_][:], in_=rbf[sb_][:])
            v.tensor_copy(out=rbb[sb_][:], in_=rbf[sb_][:]).then_inc(sv.h, 1)
        V(f)
        sv.inc()
        v1mark[(hl, qt)] = sv.n

    def e_R(hl, qt):
        sb_ = qt % 2
        def f(t, hl=hl, qt=qt, sb_=sb_):
            t.wait_ge(sv.h, v1mark[(hl, qt)])
            t.matmul(ps_u[sb_][:], ones_row[:], rbb[sb_][:],
                     start=True, stop=True).then_inc(st.h, 1)
        T(f)
        st.inc()
        rank1m[(hl, qt)] = st.n

    def e_V2(hl, qt):
        sb_ = qt % 2
        ab = astg[qt % 2]
        um = astg_w.get((hl, qt - 2)) or (astg_w.get((hl - 1, qt + 6)) if qt < 2 else None)
        def f(v, hl=hl, qt=qt, sb_=sb_, ab=ab, um=um):
            v.wait_ge(st.h, rank1m[(hl, qt)])
            if um:
                v.wait_ge(sa.h, um)
            v.tensor_copy(out=rbc[:], in_=ps_u[sb_][:]).then_inc(sv.h, 1)
            v.tensor_tensor(out=ab[:], in0=ps_o[sb_][:], in1=rbc[:], op=ALU.mult).then_inc(sv.h, 1)
        V(f)
        sv.inc(); rcopy[(hl, qt)] = sv.n
        sv.inc(); evacA[(hl, qt)] = sv.n

    def e_AST(hl, qt):
        ab = astg[qt % 2]
        am = evacA[(hl, qt)]
        def f(a, hl=hl, qt=qt, ab=ab, am=am):
            a.wait_ge(sv.h, am)
            a.dma_start(out=s_attnT[hl * 128:(hl + 1) * 128, qt * 512:(qt + 1) * 512],
                        in_=ab[:]).then_inc(sa.h, 16)
        S(f)
        sa.inc(16)
        astg_w[(hl, qt)] = sa.n

    def flush_pair(hl, qtA, qtB):
        e_R(hl, qtA); e_V2(hl, qtA); e_AST(hl, qtA)
        e_R(hl, qtB); e_V2(hl, qtB); e_AST(hl, qtB)

    for hl in range(16):
        qb, kb, vb = qh[hl % 2], kh[hl % 2], vh[hl % 2]
        def afn(a, hl=hl, qb=qb, kb=kb, vb=vb):
            um = head_done.get(hl - 2)
            if um:
                a.wait_ge(st.h, um)
            a.dma_start(out=qb[:], in_=s_qT[hl * 128:(hl + 1) * 128, :]).then_inc(sa.h, 16)
            a.dma_start(out=kb[:], in_=s_kT[hl * 128:(hl + 1) * 128, :]).then_inc(sa.h, 16)
            a.dma_start(out=vb[:], in_=s_vT[hl * 128:(hl + 1) * 128, :]).then_inc(sa.h, 16)
        S(afn)
        sa.inc(48); hm = sa.n

        if hl >= 1:
            flush_pair(hl - 1, 6, 7)

        twl = [(sa.h, hm)]
        vwl = []
        if hl >= 1:
            twl.append((sv.h, evacA[(hl - 1, 6)]))
            twl.append((sv.h, evacA[(hl - 1, 7)]))
            vwl = [(st.h, head_done[hl - 1])]
        tp_state["banks"] = [ps_o[0][:].bitcast(BF16), ps_o[1][:].bitcast(BF16)]
        _, vt_ready = emit_transposes(
            src_fn=lambda c, vb=vb: vb[:, c * 128:(c + 1) * 128], nch=32,
            evac_dst_fn=lambda g0, glen: vtok[:, g0 * 128:(g0 + glen) * 128],
            t_waits=twl, v_waits=vwl)

        for i in range(4):
            A, B = 2 * i, 2 * i + 1
            KA, KB = 4 * A + 4, 4 * B + 4
            extra0 = [(sa.h, hm)] if i == 0 else ()
            for (s_, kc) in ((A, 0), (B, 0), (A, 1)):
                g = e_S(hl, s_, kc, extra=extra0 if (s_, kc) == (A, 0) else ())
                e_EXP(hl, s_, kc, g)
                if kc >= 4 * s_:
                    e_MASK(hl, s_, kc, g)
            if i >= 1:
                flush_pair(hl, 2 * i - 2, 2 * i - 1)
            sa_n, sb_n = 2, 1
            ua, ub = 0, 0
            cur = 0
            while ua < KA or ub < KB:
                if cur == 0 and ua < KA:
                    e_UO(hl, A, ua, KA, vt_ready); ua += 1
                    if sa_n < KA:
                        g = e_S(hl, A, sa_n)
                        e_EXP(hl, A, sa_n, g)
                        if sa_n >= 4 * A:
                            e_MASK(hl, A, sa_n, g)
                        sa_n += 1
                    if ua == KA:
                        e_V1(hl, A, KA)
                elif cur == 1 and ub < KB:
                    e_UO(hl, B, ub, KB, vt_ready); ub += 1
                    if sb_n < KB:
                        g = e_S(hl, B, sb_n)
                        e_EXP(hl, B, sb_n, g)
                        if sb_n >= 4 * B:
                            e_MASK(hl, B, sb_n, g)
                        sb_n += 1
                    if ub == KB:
                        e_V1(hl, B, KB)
                cur ^= 1
        head_done[hl] = st.n

    flush_pair(15, 6, 7)
    p3.close()
    fence()
    if debug_stage == "p3":
        es.close()
        return nc

    p4 = contextlib.ExitStack()
    woT_all = p4.enter_context(nc.sbuf_tensor("woT_all", [128, 16 * D], BF16))
    aT = [p4.enter_context(nc.sbuf_tensor(f"aT{i}", [128, FO], BF16)) for i in range(2)]
    prow = [p4.enter_context(nc.sbuf_tensor(f"prow{i}", [128, D], BF16)) for i in range(2)]
    ps_b = [p4.enter_context(nc.psum_tensor(f"ps_b{i}", [128, 512], F32)) for i in range(2)]

    def yw(y):
        for j in range(16):
            y.dma_start(out=woT_all[:, j * D:(j + 1) * D],
                        in_=s_woT[j * 128:(j + 1) * 128, :]).then_inc(sy.h, 16)
    Y(yw)
    sy.inc(16 * 16); wo_ld = sy.n

    aT_ld = {}
    mm_done = {}
    evacP = {}
    prow_w = {}
    for m in range(32):
        ab = aT[m % 2]
        def afn(a, m=m, ab=ab):
            um = mm_done.get(m - 2)
            if um:
                a.wait_ge(st.h, um)
            for j in range(16):
                a.dma_start(out=ab[:, j * 128:(j + 1) * 128],
                            in_=s_attnT[j * 128:(j + 1) * 128, m * 128:(m + 1) * 128]).then_inc(sa.h, 16)
        S(afn)
        sa.inc(16 * 16); am = sa.n
        aT_ld[m] = am
        pr = prow[m % 2]
        for ot in range(8):
            bank = ps_b[ot % 2]
            def tfn(t, m=m, ot=ot, bank=bank, ab=ab, am=am):
                if m == 0 and ot == 0:
                    t.wait_ge(sy.h, wo_ld)
                t.wait_ge(sa.h, am)
                um = evacP.get((m, ot - 2)) or (evacP.get((m - 1, ot + 6)) if ot < 2 else None)
                if um:
                    t.wait_ge(sv.h, um)
                for j in range(16):
                    t.matmul(bank[:], ab[:, j * 128:(j + 1) * 128],
                             woT_all[:, j * D + ot * 512:j * D + (ot + 1) * 512],
                             start=(j == 0), stop=False)
                t.matmul(bank[:], ones_row[:], bo_bf[0:1, ot * 512:(ot + 1) * 512],
                         start=False, stop=True).then_inc(st.h, 1)
            T(tfn)
            st.inc(); mmm = st.n
            def vfn(v, m=m, ot=ot, bank=bank, pr=pr, mmm=mmm):
                v.wait_ge(st.h, mmm)
                if ot == 0:
                    um = prow_w.get(m - 2)
                    if um:
                        v.wait_ge(sg.h, um)
                v.tensor_copy(out=pr[:, ot * 512:(ot + 1) * 512], in_=bank[:]).then_inc(sv.h, 1)
            V(vfn)
            sv.inc(); evacP[(m, ot)] = sv.n
        mm_done[m] = st.n
        vm = sv.n
        def gfn(g, m=m, pr=pr, vm=vm):
            g.wait_ge(sv.h, vm)
            g.dma_start(out=s_part[m * 128:(m + 1) * 128, :], in_=pr[:]).then_inc(sg.h, 16)
        G(gfn)
        sg.inc(16); prow_w[m] = sg.n

    part_done = sg.n
    def gcc(g):
        g.wait_ge(sg.h, part_done)
        g.collective_compute(
            "ReduceScatter", ALU.add,
            replica_groups=[[0, 1], [2, 3], [4, 5], [6, 7]],
            ins=[s_part.opt()],
            outs=[s_prs.opt()],
        ).then_inc(scc.h, 1)
    G(gcc)
    scc.inc()
    p4.close()
    fence()

    p5a = contextlib.ExitStack()
    pr_b = [p5a.enter_context(nc.sbuf_tensor(f"pr_b{i}", [128, D], BF16)) for i in range(2)]
    xo_b = [p5a.enter_context(nc.sbuf_tensor(f"xo_b{i}", [128, D], F32)) for i in range(2)]
    ln2_bf = p5a.enter_context(nc.sbuf_tensor("ln2_bf", [128, D], BF16))
    x1_b = [p5a.enter_context(nc.sbuf_tensor(f"x1_b{i}", [128, D], F32)) for i in range(2)]

    norm_done = {}
    x1_w = {}
    res_done = {}
    for t_ in range(16):
        prb, xob, x1b = pr_b[t_ % 2], xo_b[t_ % 2], x1_b[t_ % 2]
        def afn(a, t_=t_, prb=prb):
            if t_ == 0:
                a.wait_ge(scc.h, 1)
            um = norm_done.get(t_ - 2)
            if um:
                a.wait_ge(ss.h, um)
            a.dma_start(out=prb[:], in_=s_prs[t_ * 128:(t_ + 1) * 128, :]).then_inc(sa.h, 16)
        S(afn)
        sa.inc(16); pm = sa.n
        def yfn(y, t_=t_, xob=xob):
            um = res_done.get(t_ - 2)
            if um:
                y.wait_ge(sv.h, um)
            y.dma_start(out=xob[:], in_=x_own[t_ * 128:(t_ + 1) * 128, :]).then_inc(sy.h, 16)
        Y(yfn)
        sy.inc(16); xm = sy.n
        def normalize(s, scl, bias, prb=prb):
            s.activation(out=ln2_bf[:], in_=prb[:], func=AF.Identity,
                         scale=scl, bias=bias).then_inc(ss.h, 1)
        emit_ln(prb[:], normalize, D, s_waits=[(sa.h, pm)], v_waits=[(sa.h, pm)])
        ss.inc(); norm_done[t_] = ss.n; nmk = ss.n
        def vres(v, t_=t_, xob=xob, x1b=x1b, nmk=nmk, xm=xm):
            v.wait_ge(ss.h, nmk)
            v.wait_ge(sy.h, xm)
            um = x1_w.get(t_ - 2)
            if um:
                v.wait_ge(sg.h, um)
            v.tensor_tensor(out=x1b[:], in0=xob[:], in1=ln2_bf[:], op=ALU.add).then_inc(sv.h, 1)
        V(vres)
        sv.inc(); res_done[t_] = sv.n; rm = sv.n
        def gfn(g, t_=t_, x1b=x1b, rm=rm):
            g.wait_ge(sv.h, rm)
            g.dma_start(out=s_x1[t_ * 128:(t_ + 1) * 128, :], in_=x1b[:]).then_inc(sg.h, 16)
        G(gfn)
        sg.inc(16); x1_w[t_] = sg.n

    p5a.close()
    fence()
    if debug_stage == "p5a":
        es.close()
        return nc

    p5b = contextlib.ExitStack()
    tp_state["banks"] = [p5b.enter_context(nc.psum_tensor(f"ps_tp5{i}", [128, 1024], BF16)) for i in range(2)]
    x1in = [p5b.enter_context(nc.sbuf_tensor(f"x1in{i}", [128, D], F32)) for i in range(2)]
    xn2_tok = [p5b.enter_context(nc.sbuf_tensor(f"xn2_tok{i}", [128, D], BF16)) for i in range(2)]
    xn2T = p5b.enter_context(nc.sbuf_tensor("xn2T", [128, 32 * 512], BF16))
    w2 = [p5b.enter_context(nc.sbuf_tensor(f"w2{i}", [128, 8 * 512], BF16)) for i in range(2)]
    fout = p5b.enter_context(nc.sbuf_tensor("fout", [128, 4 * D], BF16))
    ln4_bf = p5b.enter_context(nc.sbuf_tensor("ln4_bf", [128, D], BF16))
    ps_c = [p5b.enter_context(nc.psum_tensor(f"ps_c{i}", [128, 512], F32)) for i in range(4)]

    w2_used = {}
    ffn_evac = {}
    ln3_done = {}
    ln4_done = {}
    x1_guard = {0: [], 1: []}
    xn2_tp = {}
    out_w = {}
    mm_last = {}
    for qq in range(4):
        for t2 in range(4):
            tg = qq * 4 + t2
            xin = x1in[t2 % 2]
            ntk = xn2_tok[t2 % 2]
            gwl = list(x1_guard[t2 % 2])
            def yfn(y, tg=tg, xin=xin, gwl=gwl):
                for (h, v) in gwl:
                    y.wait_ge(h, v)
                y.dma_start(out=xin[:], in_=s_x1[tg * 128:(tg + 1) * 128, :]).then_inc(sy.h, 16)
            Y(yfn)
            sy.inc(16); xm = sy.n
            ntk_free = xn2_tp.get((qq, t2 - 2)) or (xn2_tp.get((qq - 1, t2 + 2)) if t2 < 2 else None)
            def normalize(s, scl, bias, xin=xin, ntk=ntk, ntk_free=ntk_free):
                if ntk_free:
                    s.wait_ge(st.h, ntk_free)
                s.activation(out=ntk[:], in_=xin[:], func=AF.Identity,
                             scale=scl, bias=bias).then_inc(ss.h, 1)
            emit_ln(xin[:], normalize, D, s_waits=[(sy.h, xm)], v_waits=[(sy.h, xm)])
            ss.inc(); ln3_done[(qq, t2)] = ss.n; nm2 = ss.n
            x1_guard[t2 % 2] = [(ss.h, ss.n)]
            vwl = [(st.h, mm_last[qq - 1])] if mm_last.get(qq - 1) else []
            tm, _ = emit_transposes(
                src_fn=lambda c, ntk=ntk: ntk[:, c * 128:(c + 1) * 128], nch=32,
                evac_dst_fn=lambda g0, glen, t2=t2:
                    xn2T[:].rearrange("p (c n) -> p c n", n=512)[:, g0:g0 + glen, t2 * 128:(t2 + 1) * 128],
                t_waits=[(ss.h, nm2)], v_waits=vwl)
            xn2_tp[(qq, t2)] = tm
        xn2_ready = sv.n

        for ot in range(8):
            for h2 in range(4):
                hidx = ot * 4 + h2
                wb = w2[hidx % 2]
                def yfn(y, qq=qq, ot=ot, h2=h2, wb=wb, hidx=hidx):
                    um = w2_used.get((qq, hidx - 2)) or (w2_used.get((qq - 1, hidx + 30)) if hidx < 2 else None)
                    if um:
                        y.wait_ge(st.h, um)
                    y.dma_start(out=wb[:], in_=s_w[ot * 128:(ot + 1) * 128,
                                                   h2 * 8 * 512:(h2 + 1) * 8 * 512]).then_inc(sy.h, 16)
                Y(yfn)
                sy.inc(16); wm = sy.n
                def tfn(t, qq=qq, ot=ot, h2=h2, wb=wb, wm=wm, hidx=hidx, xn2_ready=xn2_ready):
                    t.wait_ge(sy.h, wm)
                    if h2 == 0:
                        t.wait_ge(sv.h, xn2_ready)
                        um = ffn_evac.get((qq, ot - 1)) or (ffn_evac.get((qq - 1, 7)) if ot == 0 else None)
                        if um:
                            t.wait_ge(sv.h, um)
                    xr = xn2T[:].rearrange("p (c n) -> p c n", n=512)
                    ins = None
                    for ci in range(8):
                        c = h2 * 8 + ci
                        for t2 in range(4):
                            ins = t.matmul(ps_c[t2][:], xr[:, c, t2 * 128:(t2 + 1) * 128],
                                           wb[:, ci * 512:(ci + 1) * 512],
                                           start=(c == 0), stop=(c == 31))
                    ins.then_inc(st.h, 1)
                T(tfn)
                st.inc(); mmm = st.n
                w2_used[(qq, hidx)] = mmm
            def vfn(v, qq=qq, ot=ot, mmm=mmm):
                v.wait_ge(st.h, mmm)
                if ot == 0 and ln4_done.get(qq - 1):
                    v.wait_ge(ss.h, ln4_done[qq - 1])
                for t2 in range(4):
                    v.tensor_copy(out=fout[:, t2 * D + ot * 512:t2 * D + (ot + 1) * 512],
                                  in_=ps_c[t2][:]).then_inc(sv.h, 1)
            V(vfn)
            sv.inc(4); ffn_evac[(qq, ot)] = sv.n
        mm_last[qq] = st.n
        ffn_done = sv.n

        for t2 in range(4):
            tg = qq * 4 + t2
            xin = x1in[t2 % 2]
            gwl = list(x1_guard[t2 % 2])
            def yfn(y, tg=tg, xin=xin, gwl=gwl):
                for (h, v) in gwl:
                    y.wait_ge(h, v)
                y.dma_start(out=xin[:], in_=s_x1[tg * 128:(tg + 1) * 128, :]).then_inc(sy.h, 16)
            Y(yfn)
            sy.inc(16); xm = sy.n
            fslice = fout[:, t2 * D:(t2 + 1) * D]
            def normalize(s, scl, bias, fslice=fslice):
                s.activation(out=ln4_bf[:], in_=fslice, func=AF.Identity,
                             scale=scl, bias=bias).then_inc(ss.h, 1)
            emit_ln(fslice, normalize, D, s_waits=[(sv.h, ffn_done)], v_waits=[(sv.h, ffn_done)])
            ss.inc(); nm4 = ss.n
            def vres(v, xin=xin, nm4=nm4, xm=xm, qq=qq, t2=t2):
                v.wait_ge(ss.h, nm4)
                v.wait_ge(sy.h, xm)
                v.tensor_tensor(out=xin[:], in0=xin[:], in1=ln4_bf[:], op=ALU.add).then_inc(sv.h, 1)
            V(vres)
            sv.inc(); rm = sv.n
            def yout(y, tg=tg, rm=rm, xin=xin):
                y.wait_ge(sv.h, rm)
                y.dma_start(out=out[tg * 128:(tg + 1) * 128, :], in_=xin[:]).then_inc(so.h, 16)
            Y(yout)
            so.inc(16)
            out_w[(qq, t2)] = so.n
            x1_guard[t2 % 2] = [(so.h, so.n)]
        ln4_done[qq] = ss.n

    def yfinal(y):
        y.wait_ge(so.h, so.n)
    Y(yfinal)
    p5b.close()
    es.close()
    return nc


def _prep_inputs(L0_attn_x, L0_attn_Wq, L0_attn_bq, L0_attn_Wk, L0_attn_bk,
                 L0_attn_Wv, L0_attn_bv, L0_attn_Wo, L0_attn_bo,
                 L0_ffn_base_packed, L0_ffn_scales, L0_ffn_w_ortho):
    x = np.ascontiguousarray(np.asarray(L0_attn_x, dtype=np.float32))
    Wq = np.asarray(L0_attn_Wq, np.float32); Wk = np.asarray(L0_attn_Wk, np.float32)
    Wv = np.asarray(L0_attn_Wv, np.float32); Wo = np.asarray(L0_attn_Wo, np.float32)
    bqv = np.asarray(L0_attn_bq, np.float32); bkv = np.asarray(L0_attn_bk, np.float32)
    bvv = np.asarray(L0_attn_bv, np.float32); bov = np.asarray(L0_attn_bo, np.float32)
    bp = np.ascontiguousarray(np.asarray(L0_ffn_base_packed, np.int32))
    scv = np.asarray(L0_ffn_scales, np.float32).reshape(-1)
    wort = np.ascontiguousarray(np.asarray(L0_ffn_w_ortho, np.float32))
    zeros_bo = np.zeros((1, D), np.float32)
    in_maps = []
    for c in range(8):
        b, hh = c // 2, c % 2
        fo = slice(2048 * hh, 2048 * hh + 2048)
        m = {
            "xb": np.ascontiguousarray(x[b]),
            "x_own": np.ascontiguousarray(x[b, 2048 * hh:2048 * hh + 2048]),
            "wq": np.ascontiguousarray(Wq[fo]),
            "wk": np.ascontiguousarray(Wk[fo]),
            "wv": np.ascontiguousarray(Wv[fo]),
            "wo": np.ascontiguousarray(Wo[:, fo]),
            "bq": np.ascontiguousarray(bqv[fo].reshape(16, 128)),
            "bk": np.ascontiguousarray(bkv[fo].reshape(16, 128)),
            "bv": np.ascontiguousarray(bvv[fo].reshape(16, 128)),
            "bo": (np.ascontiguousarray(bov.reshape(1, D)) if hh == 0 else zeros_bo),
            "bp": bp,
            "sc": np.ascontiguousarray(scv.reshape(32, 128)),
            "wort": wort,
        }
        in_maps.append(m)
    return in_maps


def kernel(**inputs):
    in_maps = _prep_inputs(**inputs)
    nc = build()
    res = run_bass_kernel_spmd(nc, in_maps, core_ids=list(range(8)))
    outp = np.empty((4, 4096, 4096), np.float32)
    for c in range(8):
        b, hh = c // 2, c % 2
        outp[b, 2048 * hh:2048 * hh + 2048] = res.results[c]["out"]
    return outp


# version 05
# speedup vs baseline: 1.0383x; 1.0383x over previous
import contextlib
import math
import numpy as np
import concourse.bass as bass
import concourse.mybir as mybir
from concourse.bass_utils import run_bass_kernel_spmd

F32 = mybir.dt.float32
BF16 = mybir.dt.bfloat16
I32 = mybir.dt.int32
AF = mybir.ActivationFunctionType
ALU = mybir.AluOpType

NB, NO, FO, D = 4096, 2048, 2048, 4096
SCALE = 1.0 / math.sqrt(128.0)
EPS = 1e-5


def build(debug_stage=None):
    nc = bass.Bass("TRN2", target_bir_lowering=False)
    es = contextlib.ExitStack()

    xb = nc.dram_tensor("xb", [NB, D], F32, kind="ExternalInput").ap()
    x_own = nc.dram_tensor("x_own", [NO, D], F32, kind="ExternalInput").ap()
    wq_in = nc.dram_tensor("wq", [FO, D], F32, kind="ExternalInput").ap()
    wk_in = nc.dram_tensor("wk", [FO, D], F32, kind="ExternalInput").ap()
    wv_in = nc.dram_tensor("wv", [FO, D], F32, kind="ExternalInput").ap()
    wo_in = nc.dram_tensor("wo", [D, FO], F32, kind="ExternalInput").ap()
    bq_in = nc.dram_tensor("bq", [16, 128], F32, kind="ExternalInput").ap()
    bk_in = nc.dram_tensor("bk", [16, 128], F32, kind="ExternalInput").ap()
    bv_in = nc.dram_tensor("bv", [16, 128], F32, kind="ExternalInput").ap()
    bo_in = nc.dram_tensor("bo", [1, D], F32, kind="ExternalInput").ap()
    bp_in = nc.dram_tensor("bp", [D, D // 2], I32, kind="ExternalInput").ap()
    sc_in = nc.dram_tensor("sc", [32, 128], F32, kind="ExternalInput").ap()
    wort_in = nc.dram_tensor("wort", [D, D], F32, kind="ExternalInput").ap()
    out = nc.dram_tensor("out", [NO, D], F32, kind="ExternalOutput").ap()

    dbg = {"p2": ["s_qT", "s_kT", "s_vT"], "p3": ["s_attnT", "s_vtokd", "s_qT", "s_kT", "s_vT", "s_wqT", "s_wkT", "s_wvT"], "p5a": ["s_x1"]}.get(debug_stage, [])

    def scratch(name, shape, dtype):
        kind = "ExternalOutput" if name in dbg else "Internal"
        return nc.dram_tensor(name, shape, dtype, kind=kind).ap()

    s_wqT = scratch("s_wqT", [16 * 128, 32 * 128], BF16)
    s_wkT = scratch("s_wkT", [16 * 128, 32 * 128], BF16)
    s_wvT = scratch("s_wvT", [16 * 128, 32 * 128], BF16)
    s_woT = scratch("s_woT", [FO, D], BF16)
    s_w = scratch("s_w", [8 * 128, 32 * 512], BF16)
    s_qT = scratch("s_qT", [FO, NB], BF16)
    s_kT = scratch("s_kT", [FO, NB], BF16)
    s_vT = scratch("s_vT", [FO, NB], BF16)
    s_attnT = scratch("s_attnT", [FO, NB], BF16)
    s_vtokd = scratch("s_vtokd", [FO, NB], BF16) if debug_stage == "p3" else None
    s_part = nc.dram_tensor("s_part", [NB, D], BF16).ap()
    s_prs = nc.dram_tensor("s_prs", [NO, D], BF16).ap()
    s_x1 = scratch("s_x1", [NO, D], F32)

    blk = es.enter_context(nc.Block())

    class Sem:
        def __init__(self, name):
            self.h = es.enter_context(nc.semaphore(name))
            self.n = 0
        def inc(self, k=1):
            self.n += k
            return self.n

    sy = Sem("sy"); sa = Sem("sa"); sg = Sem("sg")
    st = Sem("st"); sv = Sem("sv"); ss = Sem("ss")
    scc = Sem("scc"); so = Sem("so")
    ALLS = [sy, sa, sg, st, sv, ss]

    def T(fn): blk.tensor(fn)
    def V(fn): blk.vector(fn)
    def S(fn): blk.scalar(fn)
    def G(fn): blk.gpsimd(fn)
    def Y(fn): blk.sync(fn)

    def fence():
        marks = [(s.h, s.n) for s in ALLS if s.n > 0]
        for E in (T, V, S, G, Y):
            def f(e, marks=marks):
                for h, v in marks:
                    e.wait_ge(h, v)
            E(f)

    ident = es.enter_context(nc.sbuf_tensor("ident", [128, 128], BF16))
    io_q = es.enter_context(nc.sbuf_tensor("io_q", [128, 512], F32))
    io_col = es.enter_context(nc.sbuf_tensor("io_col", [128, 1], F32))
    masks = es.enter_context(nc.sbuf_tensor("masks", [128, 4 * 512], BF16))
    thr = es.enter_context(nc.sbuf_tensor("thr", [128, 4], F32))
    ones_col = es.enter_context(nc.sbuf_tensor("ones_col", [128, 1], BF16))
    ones_row = es.enter_context(nc.sbuf_tensor("ones_row", [1, 128], BF16))
    bq_sb = es.enter_context(nc.sbuf_tensor("bq_sb", [128, 16], F32))
    bk_sb = es.enter_context(nc.sbuf_tensor("bk_sb", [128, 16], F32))
    bv_sb = es.enter_context(nc.sbuf_tensor("bv_sb", [128, 16], F32))
    bo_bf = es.enter_context(nc.sbuf_tensor("bo_bf", [1, D], BF16))
    sc_sb = es.enter_context(nc.sbuf_tensor("sc_sb", [128, 32], F32))
    m8s_sb = es.enter_context(nc.sbuf_tensor("m8s_sb", [128, 32], F32))
    junk1 = es.enter_context(nc.sbuf_tensor("junk1", [128, D], BF16))
    junk2 = es.enter_context(nc.sbuf_tensor("junk2", [128, D], BF16))
    st_sum = es.enter_context(nc.sbuf_tensor("st_sum", [128, 1], F32))
    st_sq = es.enter_context(nc.sbuf_tensor("st_sq", [128, 1], F32))
    st_mean = es.enter_context(nc.sbuf_tensor("st_mean", [128, 1], F32))
    st_var = es.enter_context(nc.sbuf_tensor("st_var", [128, 1], F32))
    st_std = es.enter_context(nc.sbuf_tensor("st_std", [128, 1], F32))
    st_rstd = es.enter_context(nc.sbuf_tensor("st_rstd", [128, 1], F32))
    st_nm = es.enter_context(nc.sbuf_tensor("st_nm", [128, 1], F32))

    tp_state = {"g": 0, "marks": {}, "banks": None}

    def emit_transposes(src_fn, nch, evac_dst_fn, t_waits=(), v_waits=()):
        ps_tp = tp_state["banks"]
        groups = [(g0, min(8, nch - g0)) for g0 in range(0, nch, 8)]
        for gi, (g0, glen) in enumerate(groups):
            g = tp_state["g"]
            bank = ps_tp[g % 2]
            prev = tp_state["marks"].get(g - 2)
            def tfn(t, gi=gi, g0=g0, glen=glen, bank=bank, prev=prev):
                if gi == 0:
                    for (h, v) in t_waits:
                        t.wait_ge(h, v)
                if prev is not None:
                    t.wait_ge(sv.h, prev)
                for k in range(glen):
                    ins = t.transpose(bank[:, k * 128:(k + 1) * 128], src_fn(g0 + k), ident[:])
                    if k == glen - 1:
                        ins.then_inc(st.h, 1)
            T(tfn)
            st.inc()
            tmark = st.n
            def vfn(v, gi=gi, g0=g0, glen=glen, bank=bank, tmark=tmark):
                if gi == 0:
                    for (h, vv) in v_waits:
                        v.wait_ge(h, vv)
                v.wait_ge(st.h, tmark)
                v.tensor_copy(out=evac_dst_fn(g0, glen), in_=bank[:, 0:glen * 128]).then_inc(sv.h, 1)
            V(vfn)
            sv.inc()
            tp_state["marks"][g] = sv.n
            tp_state["g"] += 1
        return st.n, sv.n

    def emit_ln(x_ap, normalize, n_free, s_waits=(), v_waits=()):
        inv_n = 1.0 / n_free
        def sfn(s):
            for (h, v) in s_waits:
                s.wait_ge(h, v)
            s.activation(out=junk1[:, 0:n_free], in_=x_ap, func=AF.Square,
                         accum_out=st_sq[:]).then_inc(ss.h, 1)
        S(sfn)
        ss.inc()
        sqm = ss.n
        sv0 = sv.n
        def vfn(v, sv0=sv0):
            for (h, vv) in v_waits:
                v.wait_ge(h, vv)
            v.tensor_scalar(out=junk2[:, 0:n_free], in0=x_ap, scalar1=1.0, scalar2=0.0,
                            op0=ALU.mult, op1=ALU.add, accum_out=st_sum[:]).then_inc(sv.h, 1)
            v.wait_ge(sv.h, sv0 + 1)
            v.wait_ge(ss.h, sqm)
            v.tensor_scalar(out=st_mean[:], in0=st_sum[:], scalar1=inv_n, scalar2=None, op0=ALU.mult)
            v.tensor_scalar(out=st_var[:], in0=st_sq[:], scalar1=inv_n, scalar2=EPS,
                            op0=ALU.mult, op1=ALU.add)
            v.scalar_tensor_tensor(out=st_nm[:], in0=st_mean[:], scalar=-1.0, in1=st_mean[:],
                                   op0=ALU.mult, op1=ALU.mult).then_inc(sv.h, 1)
            v.wait_ge(sv.h, sv0 + 2)
            v.tensor_tensor(out=st_var[:], in0=st_var[:], in1=st_nm[:], op=ALU.add).then_inc(sv.h, 1)
        V(vfn)
        sv.inc(3)
        vm1 = sv.n
        def sfn2(s):
            s.wait_ge(sv.h, vm1)
            s.activation(out=st_std[:], in_=st_var[:], func=AF.Sqrt).then_inc(ss.h, 1)
        S(sfn2)
        ss.inc()
        sq2 = ss.n
        sv1 = sv.n
        def vfn2(v, sv1=sv1):
            v.wait_ge(ss.h, sq2)
            v.reciprocal(out=st_rstd[:], in_=st_std[:]).then_inc(sv.h, 1)
            v.wait_ge(sv.h, sv1 + 1)
            v.scalar_tensor_tensor(out=st_nm[:], in0=st_mean[:], scalar=-1.0, in1=st_rstd[:],
                                   op0=ALU.mult, op1=ALU.mult).then_inc(sv.h, 1)
        V(vfn2)
        sv.inc(2)
        vm2 = sv.n
        def sfn3(s):
            s.wait_ge(sv.h, vm2)
            normalize(s, st_rstd[:], st_nm[:])
        S(sfn3)
        return vm2

    def g0(g):
        g.iota(io_q[:], [[1, 512]], base=0, channel_multiplier=0,
               allow_small_or_imprecise_dtypes=True)
        g.iota(io_col[:], [[0, 1]], base=0, channel_multiplier=1,
               allow_small_or_imprecise_dtypes=True)
        for d in range(4):
            g.iota(thr[:, d:d + 1], [[0, 1]], base=128 * d, channel_multiplier=1,
                   allow_small_or_imprecise_dtypes=True)
        g.memset(ones_col[:], 1.0)
        g.memset(ones_row[:], 1.0)
        for j in range(16):
            g.dma_start(out=bq_sb[:, j:j + 1], in_=bq_in[j].rearrange("(p one) -> p one", one=1)).then_inc(sg.h, 16)
            g.dma_start(out=bk_sb[:, j:j + 1], in_=bk_in[j].rearrange("(p one) -> p one", one=1)).then_inc(sg.h, 16)
            g.dma_start(out=bv_sb[:, j:j + 1], in_=bv_in[j].rearrange("(p one) -> p one", one=1)).then_inc(sg.h, 16)
        for j in range(32):
            g.dma_start(out=sc_sb[:, j:j + 1], in_=sc_in[j].rearrange("(p one) -> p one", one=1)).then_inc(sg.h, 16)
        g.dma_start(out=bo_bf[:], in_=bo_in).then_inc(sg.h, 16)
    G(g0)
    sg.inc(16 * 81)
    start_g = sg.n
    def v0(v):
        v.wait_ge(sg.h, start_g)
        v.tensor_scalar(out=ident[:], in0=io_q[:, 0:128], scalar1=io_col[:], scalar2=None, op0=ALU.is_equal)
        for d in range(4):
            v.tensor_scalar(out=masks[:, d * 512:(d + 1) * 512], in0=io_q[:],
                            scalar1=thr[:, d:d + 1], scalar2=None, op0=ALU.is_ge)
        v.tensor_scalar(out=m8s_sb[:], in0=sc_sb[:], scalar1=-8.0, scalar2=None, op0=ALU.mult).then_inc(sv.h, 1)
    V(v0)
    sv.inc()
    startup_done = sv.n

    p0 = contextlib.ExitStack()
    tp_state["banks"] = [p0.enter_context(nc.psum_tensor(f"ps_tp0{i}", [128, 1024], BF16)) for i in range(2)]
    w_in = [p0.enter_context(nc.sbuf_tensor(f"w_in{i}", [128, D], F32)) for i in range(2)]
    w_bf = [p0.enter_context(nc.sbuf_tensor(f"w_bf{i}", [128, D], BF16)) for i in range(2)]
    stg = [p0.enter_context(nc.sbuf_tensor(f"stg{i}", [128, D], BF16)) for i in range(2)]
    bp_b = [p0.enter_context(nc.sbuf_tensor(f"bp_b{i}", [128, D // 2], I32)) for i in range(2)]
    ort_b = [p0.enter_context(nc.sbuf_tensor(f"ort_b{i}", [128, D], F32)) for i in range(2)]
    lo_sb = p0.enter_context(nc.sbuf_tensor("lo_sb", [128, D // 2], I32))
    hi_sb = p0.enter_context(nc.sbuf_tensor("hi_sb", [128, D // 2], I32))
    dq_e = p0.enter_context(nc.sbuf_tensor("dq_e", [128, D // 2], F32))
    dq_o = p0.enter_context(nc.sbuf_tensor("dq_o", [128, D // 2], F32))

    P0G = {"win": {}, "wbf": {}, "stg": {}, "bpb": {}}

    def p0_std(w_ap, dst):
        for j in range(16):
            binp, bbf, bstg = w_in[j % 2], w_bf[j % 2], stg[j % 2]
            def yfn(y, j=j, binp=binp):
                pm = P0G["win"].get(j % 2)
                if pm:
                    y.wait_ge(pm[0], pm[1])
                y.dma_start(out=binp[:], in_=w_ap[j * 128:(j + 1) * 128, :]).then_inc(sy.h, 16)
            Y(yfn)
            sy.inc(16); ym = sy.n
            wbf_pm = P0G["wbf"].get(j % 2)
            def sfn(s, j=j, binp=binp, bbf=bbf, ym=ym, wbf_pm=wbf_pm):
                s.wait_ge(sy.h, ym)
                if wbf_pm:
                    s.wait_ge(wbf_pm[0], wbf_pm[1])
                s.activation(out=bbf[:], in_=binp[:], func=AF.Identity).then_inc(ss.h, 1)
            S(sfn)
            ss.inc(); cm = ss.n
            P0G["win"][j % 2] = (ss.h, cm)
            twl = [(ss.h, cm)] + ([(sv.h, startup_done)] if j == 0 else [])
            spm = P0G["stg"].get(j % 2)
            vwl = [spm] if spm else []
            tm, vm = emit_transposes(
                src_fn=lambda c, bbf=bbf: bbf[:, c * 128:(c + 1) * 128], nch=32,
                evac_dst_fn=lambda g0, glen, bstg=bstg: bstg[:, g0 * 128:(g0 + glen) * 128],
                t_waits=twl, v_waits=vwl)
            P0G["wbf"][j % 2] = (st.h, tm)
            def gfn(g, j=j, bstg=bstg, vm=vm, dst=dst):
                g.wait_ge(sv.h, vm)
                g.dma_start(out=dst[j * 128:(j + 1) * 128, :], in_=bstg[:]).then_inc(sg.h, 16)
            G(gfn)
            sg.inc(16)
            P0G["stg"][j % 2] = (sg.h, sg.n)

    def p0_wo():
        for j in range(16):
            binp, bbf, bstg = w_in[j % 2], w_bf[j % 2], stg[j % 2]
            def yfn(y, j=j, binp=binp):
                pm = P0G["win"].get(j % 2)
                if pm:
                    y.wait_ge(pm[0], pm[1])
                for r in range(32):
                    y.dma_start(out=binp[:, r * 128:(r + 1) * 128],
                                in_=wo_in[r * 128:(r + 1) * 128, j * 128:(j + 1) * 128]
                                ).then_inc(sy.h, 16)
            Y(yfn)
            sy.inc(16 * 32); ym = sy.n
            wbf_pm = P0G["wbf"].get(j % 2)
            def sfn(s, j=j, binp=binp, bbf=bbf, ym=ym, wbf_pm=wbf_pm):
                s.wait_ge(sy.h, ym)
                if wbf_pm:
                    s.wait_ge(wbf_pm[0], wbf_pm[1])
                s.activation(out=bbf[:], in_=binp[:], func=AF.Identity).then_inc(ss.h, 1)
            S(sfn)
            ss.inc(); cm = ss.n
            P0G["win"][j % 2] = (ss.h, cm)
            spm = P0G["stg"].get(j % 2)
            vwl = [spm] if spm else []
            tm, vm = emit_transposes(
                src_fn=lambda c, bbf=bbf: bbf[:, c * 128:(c + 1) * 128], nch=32,
                evac_dst_fn=lambda g0, glen, bstg=bstg: bstg[:, g0 * 128:(g0 + glen) * 128],
                t_waits=[(ss.h, cm)], v_waits=vwl)
            P0G["wbf"][j % 2] = (st.h, tm)
            def gfn(g, j=j, bstg=bstg, vm=vm):
                g.wait_ge(sv.h, vm)
                g.dma_start(out=s_woT[j * 128:(j + 1) * 128, :], in_=bstg[:]).then_inc(sg.h, 16)
            G(gfn)
            sg.inc(16)
            P0G["stg"][j % 2] = (sg.h, sg.n)

    def p0_ffn():
        for r in range(32):
            bbp, bor, bbf, bstg = bp_b[r % 2], ort_b[r % 2], w_bf[r % 2], stg[r % 2]
            def yfn(y, r=r, bbp=bbp, bor=bor):
                pm = P0G["bpb"].get(r % 2)
                if pm:
                    y.wait_ge(pm[0], pm[1])
                y.dma_start(out=bbp[:], in_=bp_in[r * 128:(r + 1) * 128, :]).then_inc(sy.h, 16)
                y.dma_start(out=bor[:], in_=wort_in[r * 128:(r + 1) * 128, :]).then_inc(sy.h, 16)
            Y(yfn)
            sy.inc(32); ym = sy.n
            wbf_pm = P0G["wbf"].get(r % 2)
            def vfn(v, r=r, bbp=bbp, bor=bor, bbf=bbf, ym=ym, wbf_pm=wbf_pm):
                v.wait_ge(sy.h, ym)
                if wbf_pm:
                    v.wait_ge(wbf_pm[0], wbf_pm[1])
                v.tensor_scalar(out=lo_sb[:], in0=bbp[:], scalar1=15, scalar2=None, op0=ALU.bitwise_and)
                v.tensor_scalar(out=hi_sb[:], in0=bbp[:], scalar1=4, scalar2=None, op0=ALU.arith_shift_right)
                v.tensor_scalar(out=dq_e[:], in0=lo_sb[:], scalar1=sc_sb[:, r:r + 1],
                                scalar2=m8s_sb[:, r:r + 1], op0=ALU.mult, op1=ALU.add)
                v.tensor_scalar(out=dq_o[:], in0=hi_sb[:], scalar1=sc_sb[:, r:r + 1],
                                scalar2=m8s_sb[:, r:r + 1], op0=ALU.mult, op1=ALU.add)
                v.tensor_tensor(out=bbf[:, 0:D:2], in0=dq_e[:], in1=bor[:, 0:D:2], op=ALU.add)
                v.tensor_tensor(out=bbf[:, 1:D:2], in0=dq_o[:], in1=bor[:, 1:D:2], op=ALU.add).then_inc(sv.h, 1)
            V(vfn)
            sv.inc(); cm = sv.n
            P0G["bpb"][r % 2] = (sv.h, cm)
            spm = P0G["stg"].get(r % 2)
            vwl = [spm] if spm else []
            tm, vm = emit_transposes(
                src_fn=lambda c, bbf=bbf: bbf[:, c * 128:(c + 1) * 128], nch=32,
                evac_dst_fn=lambda g0, glen, bstg=bstg: bstg[:, g0 * 128:(g0 + glen) * 128],
                t_waits=[(sv.h, cm)], v_waits=vwl)
            P0G["wbf"][r % 2] = (st.h, tm)
            ot, rr = r // 4, r % 4
            def gfn(g, r=r, bstg=bstg, vm=vm, ot=ot, rr=rr):
                g.wait_ge(sv.h, vm)
                dstblk = s_w[ot * 128:(ot + 1) * 128, :].rearrange("p (c o) -> p c o", o=512)
                dst = dstblk[:, :, rr * 128:(rr + 1) * 128]
                srcb = bstg[:].rearrange("p (c o) -> p c o", o=128)
                for h4 in range(4):
                    g.dma_start(out=dst[:, h4 * 8:(h4 + 1) * 8, :],
                                in_=srcb[:, h4 * 8:(h4 + 1) * 8, :],
                                allow_slow_non_contiguous=True).then_inc(sg.h, 16)
            G(gfn)
            sg.inc(64)
            P0G["stg"][r % 2] = (sg.h, sg.n)

    p0_std(wq_in, s_wqT)
    p0_std(wk_in, s_wkT)
    p0_std(wv_in, s_wvT)
    p0_wo()
    p0_ffn()
    p0.close()
    fence()

    p2 = contextlib.ExitStack()
    tp_state["banks"] = [p2.enter_context(nc.psum_tensor(f"ps_tp2{i}", [128, 1024], BF16)) for i in range(2)]
    x_b = [p2.enter_context(nc.sbuf_tensor(f"x_b{i}", [128, D], F32)) for i in range(2)]
    xn_tok = [p2.enter_context(nc.sbuf_tensor(f"xn_tok{i}", [128, D], BF16)) for i in range(2)]
    xnT = [p2.enter_context(nc.sbuf_tensor(f"xnT{i}", [128, 32 * 512], BF16)) for i in range(2)]
    wblk = [p2.enter_context(nc.sbuf_tensor(f"wblk{i}", [128, D], BF16)) for i in range(2)]
    pstg = [p2.enter_context(nc.sbuf_tensor(f"pstg{i}", [128, 512], BF16)) for i in range(2)]
    ps_mm = [p2.enter_context(nc.psum_tensor(f"ps_mm{i}", [128, 512], F32)) for i in range(2)]

    WLIST = [("q", s_wqT, s_qT, bq_sb), ("k", s_wkT, s_kT, bk_sb), ("v", s_wvT, s_vT, bv_sb)]
    xn_used = {}
    xn_tok_tp = {}
    wblk_used = {}
    evac_marks = {}
    pstg_w = {}
    xload = {}
    widx = 0

    for e in range(8):
        nbuf = xnT[e % 2]
        for tci in range(4):
            tc = e * 4 + tci
            xbuf = x_b[tci % 2]
            ntk = xn_tok[tci % 2]
            gtc = e * 4 + tci
            def yfn(y, e=e, tci=tci, tc=tc, xbuf=xbuf):
                pm = xload.get((e, tci - 2)) or (xload.get((e - 1, tci + 2)) if tci < 2 else None)
                if pm:
                    y.wait_ge(ss.h, pm)
                y.dma_start(out=xbuf[:], in_=xb[tc * 128:(tc + 1) * 128, :]).then_inc(sy.h, 16)
            Y(yfn)
            sy.inc(16); ym = sy.n
            ntk_free = xn_tok_tp.get(gtc - 2)
            def normalize(s, scl, bias, xbuf=xbuf, ntk=ntk, ntk_free=ntk_free):
                if ntk_free:
                    s.wait_ge(st.h, ntk_free)
                s.activation(out=ntk[:], in_=xbuf[:], func=AF.Identity,
                             scale=scl, bias=bias).then_inc(ss.h, 1)
            emit_ln(xbuf[:], normalize, D, s_waits=[(sy.h, ym)], v_waits=[(sy.h, ym)])
            ss.inc()
            xload[(e, tci)] = ss.n
            nm = ss.n
            vwl = [(st.h, xn_used[e - 2])] if xn_used.get(e - 2) else []
            tm, _ = emit_transposes(
                src_fn=lambda c, ntk=ntk: ntk[:, c * 128:(c + 1) * 128], nch=32,
                evac_dst_fn=lambda g0, glen, nbuf=nbuf, tci=tci:
                    nbuf[:].rearrange("p (c n) -> p c n", n=512)[:, g0:g0 + glen, tci * 128:(tci + 1) * 128],
                t_waits=[(ss.h, nm)], v_waits=vwl)
            xn_tok_tp[gtc] = tm
        xnT_ready = sv.n

        for (wname, wscr, dscr, bias_sb) in WLIST:
            for j in range(16):
                wb = wblk[widx % 2]
                def yfn(y, wb=wb, wscr=wscr, j=j, widx=widx):
                    um = wblk_used.get(widx - 2)
                    if um:
                        y.wait_ge(st.h, um)
                    y.dma_start(out=wb[:], in_=wscr[j * 128:(j + 1) * 128, :]).then_inc(sy.h, 16)
                Y(yfn)
                sy.inc(16); wm = sy.n
                bank = ps_mm[widx % 2]
                def tfn(t, wb=wb, bank=bank, wm=wm, widx=widx, nbuf=nbuf, xnT_ready=xnT_ready):
                    t.wait_ge(sy.h, wm)
                    t.wait_ge(sv.h, xnT_ready)
                    um = evac_marks.get(widx - 2)
                    if um:
                        t.wait_ge(ss.h, um)
                    rhs = nbuf[:].rearrange("p (c n) -> p c n", n=512)
                    ins = None
                    for c in range(32):
                        ins = t.matmul(bank[:], wb[:, c * 128:(c + 1) * 128], rhs[:, c, :],
                                       start=(c == 0), stop=(c == 31))
                    ins.then_inc(st.h, 1)
                T(tfn)
                st.inc(); mm = st.n
                wblk_used[widx] = mm
                pbuf = pstg[widx % 2]
                def sfn(s, bank=bank, pbuf=pbuf, mm=mm, bias_sb=bias_sb, j=j, widx=widx):
                    s.wait_ge(st.h, mm)
                    um = pstg_w.get(widx - 2)
                    if um:
                        s.wait_ge(sa.h, um)
                    s.activation(out=pbuf[:], in_=bank[:], func=AF.Identity,
                                 bias=bias_sb[:, j:j + 1], scale=1.0).then_inc(ss.h, 1)
                S(sfn)
                ss.inc(); em = ss.n
                evac_marks[widx] = em
                def afn(a, pbuf=pbuf, dscr=dscr, j=j, e=e, em=em):
                    a.wait_ge(ss.h, em)
                    a.dma_start(out=dscr[j * 128:(j + 1) * 128, e * 512:(e + 1) * 512],
                                in_=pbuf[:]).then_inc(sa.h, 16)
                S(afn)
                sa.inc(16)
                pstg_w[widx] = sa.n
                widx += 1
        xn_used[e] = st.n

    p2.close()
    fence()
    if debug_stage == "p2":
        es.close()
        return nc

    p3 = contextlib.ExitStack()
    qh = [p3.enter_context(nc.sbuf_tensor(f"qh{i}", [128, NB], BF16)) for i in range(2)]
    kh = [p3.enter_context(nc.sbuf_tensor(f"kh{i}", [128, NB], BF16)) for i in range(2)]
    vh = [p3.enter_context(nc.sbuf_tensor(f"vh{i}", [128, NB], BF16)) for i in range(2)]
    vtok = p3.enter_context(nc.sbuf_tensor("vtok", [128, NB], BF16))
    p_sb = [p3.enter_context(nc.sbuf_tensor(f"p_sb{i}", [128, 512], BF16)) for i in range(4)]
    rbf = [p3.enter_context(nc.sbuf_tensor(f"rbf{i}", [1, 512], F32)) for i in range(2)]
    rbb = [p3.enter_context(nc.sbuf_tensor(f"rbb{i}", [1, 512], BF16)) for i in range(2)]
    rbc = p3.enter_context(nc.sbuf_tensor("rbc", [128, 512], F32))
    astg = [p3.enter_context(nc.sbuf_tensor(f"astg{i}", [128, 512], BF16)) for i in range(2)]
    ps_s = [p3.enter_context(nc.psum_tensor(f"ps_s{i}", [128, 512], F32)) for i in range(2)]
    ps_tp3 = [p3.enter_context(nc.psum_tensor(f"ps_tp3{i}", [128, 1024], BF16)) for i in range(2)]
    ps_o = [p3.enter_context(nc.psum_tensor(f"ps_o{i}", [128, 512], F32)) for i in range(2)]
    ps_u = [p3.enter_context(nc.psum_tensor(f"ps_u{i}", [128, 512], F32)) for i in range(2)]

    head_done = {}
    smark = {}
    uomark = {}
    expmark = {}
    maskmark = {}
    v1mark = {}
    rank1m = {}
    rcopy = {}
    evacA = {}
    astg_w = {}
    gS = [0]
    expglobal = {}
    gidx = {}

    def e_S(hl, qt, kc, extra=()):
        g = gS[0]
        bank = g % 2
        pgu = expglobal.get(g - 2)
        def f(t, hl=hl, qt=qt, kc=kc, bank=bank, pgu=pgu, extra=tuple(extra)):
            for (h, v) in extra:
                t.wait_ge(h, v)
            if pgu:
                t.wait_ge(ss.h, expmark[pgu])
            t.matmul(ps_s[bank][:], kh[hl % 2][:, kc * 128:(kc + 1) * 128],
                     qh[hl % 2][:, qt * 512:(qt + 1) * 512],
                     start=True, stop=True).then_inc(st.h, 1)
        T(f)
        st.inc()
        smark[(hl, qt, kc)] = st.n
        expglobal[g] = (hl, qt, kc)
        gidx[(hl, qt, kc)] = g
        gS[0] += 1
        return g

    def e_EXP(hl, qt, kc, g):
        bank = g % 2
        buf = g % 4
        pb = expglobal.get(g - 4)
        def f(s, hl=hl, qt=qt, kc=kc, bank=bank, buf=buf, pb=pb):
            s.wait_ge(st.h, smark[(hl, qt, kc)])
            if pb is not None:
                s.wait_ge(st.h, uomark[pb])
            s.activation(out=p_sb[buf][:], in_=ps_s[bank][:], func=AF.Exp,
                         scale=SCALE).then_inc(ss.h, 1)
        S(f)
        ss.inc()
        expmark[(hl, qt, kc)] = ss.n

    def e_MASK(hl, qt, kc, g):
        buf = g % 4
        dd = kc - 4 * qt
        def f(v, hl=hl, qt=qt, kc=kc, buf=buf, dd=dd):
            v.wait_ge(ss.h, expmark[(hl, qt, kc)])
            v.tensor_tensor(out=p_sb[buf][:], in0=p_sb[buf][:],
                            in1=masks[:, dd * 512:(dd + 1) * 512], op=ALU.mult).then_inc(sv.h, 1)
        V(f)
        sv.inc()
        maskmark[(hl, qt, kc)] = sv.n

    def e_UO(hl, qt, kc, K, vt_ready):
        buf = gidx[(hl, qt, kc)] % 4
        sb_ = qt % 2
        diag = kc >= 4 * qt
        first, last = kc == 0, kc == K - 1
        prev_same = (hl, qt - 2) if qt >= 2 else ((hl - 1, qt + 6) if hl >= 1 else None)
        def f(t, hl=hl, qt=qt, kc=kc, buf=buf, sb_=sb_, diag=diag, first=first,
              last=last, prev_same=prev_same, vt_ready=vt_ready):
            if diag:
                t.wait_ge(sv.h, maskmark[(hl, qt, kc)])
            else:
                t.wait_ge(ss.h, expmark[(hl, qt, kc)])
            if first:
                t.wait_ge(sv.h, vt_ready)
                if prev_same:
                    if prev_same in rcopy:
                        t.wait_ge(sv.h, rcopy[prev_same])
                    if prev_same in evacA:
                        t.wait_ge(sv.h, evacA[prev_same])
            t.matmul(ps_u[sb_][0:1, :], ones_col[:], p_sb[buf][:],
                     start=first, stop=last)
            t.matmul(ps_o[sb_][:], vtok[:, kc * 128:(kc + 1) * 128], p_sb[buf][:],
                     start=first, stop=last).then_inc(st.h, 1)
        T(f)
        st.inc()
        uomark[(hl, qt, kc)] = st.n

    def e_V1(hl, qt, K):
        sb_ = qt % 2
        prev_rb = (hl, qt - 2) if qt >= 2 else ((hl - 1, qt + 6) if hl >= 1 else None)
        def f(v, hl=hl, qt=qt, K=K, sb_=sb_, prev_rb=prev_rb):
            v.wait_ge(st.h, uomark[(hl, qt, K - 1)])
            if prev_rb and prev_rb in rank1m:
                v.wait_ge(st.h, rank1m[prev_rb])
            v.tensor_copy(out=rbf[sb_][:], in_=ps_u[sb_][0:1, :])
            v.reciprocal(out=rbf[sb_][:], in_=rbf[sb_][:])
            v.tensor_copy(out=rbb[sb_][:], in_=rbf[sb_][:]).then_inc(sv.h, 1)
        V(f)
        sv.inc()
        v1mark[(hl, qt)] = sv.n

    def e_R(hl, qt):
        sb_ = qt % 2
        def f(t, hl=hl, qt=qt, sb_=sb_):
            t.wait_ge(sv.h, v1mark[(hl, qt)])
            t.matmul(ps_u[sb_][:], ones_row[:], rbb[sb_][:],
                     start=True, stop=True).then_inc(st.h, 1)
        T(f)
        st.inc()
        rank1m[(hl, qt)] = st.n

    def e_V2(hl, qt):
        sb_ = qt % 2
        ab = astg[qt % 2]
        um = astg_w.get((hl, qt - 2)) or (astg_w.get((hl - 1, qt + 6)) if qt < 2 else None)
        def f(v, hl=hl, qt=qt, sb_=sb_, ab=ab, um=um):
            v.wait_ge(st.h, rank1m[(hl, qt)])
            if um:
                v.wait_ge(sa.h, um)
            v.tensor_copy(out=rbc[:], in_=ps_u[sb_][:]).then_inc(sv.h, 1)
            v.tensor_tensor(out=ab[:], in0=ps_o[sb_][:], in1=rbc[:], op=ALU.mult).then_inc(sv.h, 1)
        V(f)
        sv.inc(); rcopy[(hl, qt)] = sv.n
        sv.inc(); evacA[(hl, qt)] = sv.n

    def e_AST(hl, qt):
        ab = astg[qt % 2]
        am = evacA[(hl, qt)]
        def f(a, hl=hl, qt=qt, ab=ab, am=am):
            a.wait_ge(sv.h, am)
            a.dma_start(out=s_attnT[hl * 128:(hl + 1) * 128, qt * 512:(qt + 1) * 512],
                        in_=ab[:]).then_inc(sa.h, 16)
        S(f)
        sa.inc(16)
        astg_w[(hl, qt)] = sa.n

    def flush_pair(hl, qtA, qtB):
        e_R(hl, qtA); e_V2(hl, qtA); e_AST(hl, qtA)
        e_R(hl, qtB); e_V2(hl, qtB); e_AST(hl, qtB)

    for hl in range(16):
        qb, kb, vb = qh[hl % 2], kh[hl % 2], vh[hl % 2]
        def afn(a, hl=hl, qb=qb, kb=kb, vb=vb):
            um = head_done.get(hl - 2)
            if um:
                a.wait_ge(st.h, um)
            a.dma_start(out=qb[:], in_=s_qT[hl * 128:(hl + 1) * 128, :]).then_inc(sa.h, 16)
            a.dma_start(out=kb[:], in_=s_kT[hl * 128:(hl + 1) * 128, :]).then_inc(sa.h, 16)
            a.dma_start(out=vb[:], in_=s_vT[hl * 128:(hl + 1) * 128, :]).then_inc(sa.h, 16)
        S(afn)
        sa.inc(48); hm = sa.n

        if hl >= 1:
            flush_pair(hl - 1, 6, 7)

        twl = [(sa.h, hm)]
        vwl = []
        if hl >= 1:
            twl.append((sv.h, evacA[(hl - 1, 6)]))
            twl.append((sv.h, evacA[(hl - 1, 7)]))
            vwl = [(st.h, head_done[hl - 1])]
        tp_state["banks"] = ps_tp3
        _, vt_ready = emit_transposes(
            src_fn=lambda c, vb=vb: vb[:, c * 128:(c + 1) * 128], nch=32,
            evac_dst_fn=lambda g0, glen: vtok[:, g0 * 128:(g0 + glen) * 128],
            t_waits=twl, v_waits=vwl)
        if s_vtokd is not None:
            def gdump(g, hl=hl, vt_ready=vt_ready):
                g.wait_ge(sv.h, vt_ready)
                g.dma_start(out=s_vtokd[hl * 128:(hl + 1) * 128, :], in_=vtok[:]).then_inc(sg.h, 16)
            G(gdump)
            sg.inc(16)

        for i in range(4):
            A, B = 2 * i, 2 * i + 1
            KA, KB = 4 * A + 4, 4 * B + 4
            extra0 = [(sa.h, hm)] if i == 0 else ()
            for (s_, kc) in ((A, 0), (B, 0)):
                g = e_S(hl, s_, kc, extra=extra0 if (s_, kc) == (A, 0) else ())
                e_EXP(hl, s_, kc, g)
                if kc >= 4 * s_:
                    e_MASK(hl, s_, kc, g)
            if i >= 1:
                flush_pair(hl, 2 * i - 2, 2 * i - 1)
            sa_n, sb_n = 1, 1
            ua, ub = 0, 0
            cur = 0
            while ua < KA or ub < KB:
                if cur == 0 and ua < KA:
                    e_UO(hl, A, ua, KA, vt_ready); ua += 1
                    if sa_n < KA:
                        g = e_S(hl, A, sa_n)
                        e_EXP(hl, A, sa_n, g)
                        if sa_n >= 4 * A:
                            e_MASK(hl, A, sa_n, g)
                        sa_n += 1
                    if ua == KA:
                        e_V1(hl, A, KA)
                elif cur == 1 and ub < KB:
                    e_UO(hl, B, ub, KB, vt_ready); ub += 1
                    if sb_n < KB:
                        g = e_S(hl, B, sb_n)
                        e_EXP(hl, B, sb_n, g)
                        if sb_n >= 4 * B:
                            e_MASK(hl, B, sb_n, g)
                        sb_n += 1
                    if ub == KB:
                        e_V1(hl, B, KB)
                cur ^= 1
        head_done[hl] = st.n

    flush_pair(15, 6, 7)
    p3.close()
    fence()
    if debug_stage == "p3":
        es.close()
        return nc

    p4 = contextlib.ExitStack()
    woT_all = p4.enter_context(nc.sbuf_tensor("woT_all", [128, 16 * D], BF16))
    aT = [p4.enter_context(nc.sbuf_tensor(f"aT{i}", [128, FO], BF16)) for i in range(2)]
    prow = [p4.enter_context(nc.sbuf_tensor(f"prow{i}", [128, D], BF16)) for i in range(2)]
    ps_b = [p4.enter_context(nc.psum_tensor(f"ps_b{i}", [128, 512], F32)) for i in range(2)]

    def yw(y):
        for j in range(16):
            y.dma_start(out=woT_all[:, j * D:(j + 1) * D],
                        in_=s_woT[j * 128:(j + 1) * 128, :]).then_inc(sy.h, 16)
    Y(yw)
    sy.inc(16 * 16); wo_ld = sy.n

    aT_ld = {}
    mm_done = {}
    evacP = {}
    prow_w = {}
    for m in range(32):
        ab = aT[m % 2]
        def afn(a, m=m, ab=ab):
            um = mm_done.get(m - 2)
            if um:
                a.wait_ge(st.h, um)
            for j in range(16):
                a.dma_start(out=ab[:, j * 128:(j + 1) * 128],
                            in_=s_attnT[j * 128:(j + 1) * 128, m * 128:(m + 1) * 128]).then_inc(sa.h, 16)
        S(afn)
        sa.inc(16 * 16); am = sa.n
        aT_ld[m] = am
        pr = prow[m % 2]
        for ot in range(8):
            bank = ps_b[ot % 2]
            def tfn(t, m=m, ot=ot, bank=bank, ab=ab, am=am):
                if m == 0 and ot == 0:
                    t.wait_ge(sy.h, wo_ld)
                t.wait_ge(sa.h, am)
                um = evacP.get((m, ot - 2)) or (evacP.get((m - 1, ot + 6)) if ot < 2 else None)
                if um:
                    t.wait_ge(sv.h, um)
                for j in range(16):
                    t.matmul(bank[:], ab[:, j * 128:(j + 1) * 128],
                             woT_all[:, j * D + ot * 512:j * D + (ot + 1) * 512],
                             start=(j == 0), stop=False)
                t.matmul(bank[:], ones_row[:], bo_bf[0:1, ot * 512:(ot + 1) * 512],
                         start=False, stop=True).then_inc(st.h, 1)
            T(tfn)
            st.inc(); mmm = st.n
            def vfn(v, m=m, ot=ot, bank=bank, pr=pr, mmm=mmm):
                v.wait_ge(st.h, mmm)
                if ot == 0:
                    um = prow_w.get(m - 2)
                    if um:
                        v.wait_ge(sg.h, um)
                v.tensor_copy(out=pr[:, ot * 512:(ot + 1) * 512], in_=bank[:]).then_inc(sv.h, 1)
            V(vfn)
            sv.inc(); evacP[(m, ot)] = sv.n
        mm_done[m] = st.n
        vm = sv.n
        def gfn(g, m=m, pr=pr, vm=vm):
            g.wait_ge(sv.h, vm)
            g.dma_start(out=s_part[m * 128:(m + 1) * 128, :], in_=pr[:]).then_inc(sg.h, 16)
        G(gfn)
        sg.inc(16); prow_w[m] = sg.n

    part_done = sg.n
    def gcc(g):
        g.wait_ge(sg.h, part_done)
        g.collective_compute(
            "ReduceScatter", ALU.add,
            replica_groups=[[0, 1], [2, 3], [4, 5], [6, 7]],
            ins=[s_part.opt()],
            outs=[s_prs.opt()],
        ).then_inc(scc.h, 1)
    G(gcc)
    scc.inc()
    p4.close()
    fence()

    p5a = contextlib.ExitStack()
    pr_b = [p5a.enter_context(nc.sbuf_tensor(f"pr_b{i}", [128, D], BF16)) for i in range(2)]
    xo_b = [p5a.enter_context(nc.sbuf_tensor(f"xo_b{i}", [128, D], F32)) for i in range(2)]
    ln2_bf = p5a.enter_context(nc.sbuf_tensor("ln2_bf", [128, D], BF16))
    x1_b = [p5a.enter_context(nc.sbuf_tensor(f"x1_b{i}", [128, D], F32)) for i in range(2)]

    norm_done = {}
    x1_w = {}
    res_done = {}
    for t_ in range(16):
        prb, xob, x1b = pr_b[t_ % 2], xo_b[t_ % 2], x1_b[t_ % 2]
        def afn(a, t_=t_, prb=prb):
            if t_ == 0:
                a.wait_ge(scc.h, 1)
            um = norm_done.get(t_ - 2)
            if um:
                a.wait_ge(ss.h, um)
            a.dma_start(out=prb[:], in_=s_prs[t_ * 128:(t_ + 1) * 128, :]).then_inc(sa.h, 16)
        S(afn)
        sa.inc(16); pm = sa.n
        def yfn(y, t_=t_, xob=xob):
            um = res_done.get(t_ - 2)
            if um:
                y.wait_ge(sv.h, um)
            y.dma_start(out=xob[:], in_=x_own[t_ * 128:(t_ + 1) * 128, :]).then_inc(sy.h, 16)
        Y(yfn)
        sy.inc(16); xm = sy.n
        def normalize(s, scl, bias, prb=prb):
            s.activation(out=ln2_bf[:], in_=prb[:], func=AF.Identity,
                         scale=scl, bias=bias).then_inc(ss.h, 1)
        emit_ln(prb[:], normalize, D, s_waits=[(sa.h, pm)], v_waits=[(sa.h, pm)])
        ss.inc(); norm_done[t_] = ss.n; nmk = ss.n
        def vres(v, t_=t_, xob=xob, x1b=x1b, nmk=nmk, xm=xm):
            v.wait_ge(ss.h, nmk)
            v.wait_ge(sy.h, xm)
            um = x1_w.get(t_ - 2)
            if um:
                v.wait_ge(sg.h, um)
            v.tensor_tensor(out=x1b[:], in0=xob[:], in1=ln2_bf[:], op=ALU.add).then_inc(sv.h, 1)
        V(vres)
        sv.inc(); res_done[t_] = sv.n; rm = sv.n
        def gfn(g, t_=t_, x1b=x1b, rm=rm):
            g.wait_ge(sv.h, rm)
            g.dma_start(out=s_x1[t_ * 128:(t_ + 1) * 128, :], in_=x1b[:]).then_inc(sg.h, 16)
        G(gfn)
        sg.inc(16); x1_w[t_] = sg.n

    p5a.close()
    fence()
    if debug_stage == "p5a":
        es.close()
        return nc

    p5b = contextlib.ExitStack()
    tp_state["banks"] = [p5b.enter_context(nc.psum_tensor(f"ps_tp5{i}", [128, 1024], BF16)) for i in range(2)]
    x1in = [p5b.enter_context(nc.sbuf_tensor(f"x1in{i}", [128, D], F32)) for i in range(2)]
    xn2_tok = [p5b.enter_context(nc.sbuf_tensor(f"xn2_tok{i}", [128, D], BF16)) for i in range(2)]
    xn2T = p5b.enter_context(nc.sbuf_tensor("xn2T", [128, 32 * 512], BF16))
    w2 = [p5b.enter_context(nc.sbuf_tensor(f"w2{i}", [128, 8 * 512], BF16)) for i in range(2)]
    fout = p5b.enter_context(nc.sbuf_tensor("fout", [128, 4 * D], BF16))
    ln4_bf = p5b.enter_context(nc.sbuf_tensor("ln4_bf", [128, D], BF16))
    ps_c = [p5b.enter_context(nc.psum_tensor(f"ps_c{i}", [128, 512], F32)) for i in range(4)]

    w2_used = {}
    ffn_evac = {}
    ln3_done = {}
    ln4_done = {}
    x1_guard = {0: [], 1: []}
    xn2_tp = {}
    out_w = {}
    mm_last = {}
    for qq in range(4):
        for t2 in range(4):
            tg = qq * 4 + t2
            xin = x1in[t2 % 2]
            ntk = xn2_tok[t2 % 2]
            gwl = list(x1_guard[t2 % 2])
            def yfn(y, tg=tg, xin=xin, gwl=gwl):
                for (h, v) in gwl:
                    y.wait_ge(h, v)
                y.dma_start(out=xin[:], in_=s_x1[tg * 128:(tg + 1) * 128, :]).then_inc(sy.h, 16)
            Y(yfn)
            sy.inc(16); xm = sy.n
            ntk_free = xn2_tp.get((qq, t2 - 2)) or (xn2_tp.get((qq - 1, t2 + 2)) if t2 < 2 else None)
            def normalize(s, scl, bias, xin=xin, ntk=ntk, ntk_free=ntk_free):
                if ntk_free:
                    s.wait_ge(st.h, ntk_free)
                s.activation(out=ntk[:], in_=xin[:], func=AF.Identity,
                             scale=scl, bias=bias).then_inc(ss.h, 1)
            emit_ln(xin[:], normalize, D, s_waits=[(sy.h, xm)], v_waits=[(sy.h, xm)])
            ss.inc(); ln3_done[(qq, t2)] = ss.n; nm2 = ss.n
            x1_guard[t2 % 2] = [(ss.h, ss.n)]
            vwl = [(st.h, mm_last[qq - 1])] if mm_last.get(qq - 1) else []
            tm, _ = emit_transposes(
                src_fn=lambda c, ntk=ntk: ntk[:, c * 128:(c + 1) * 128], nch=32,
                evac_dst_fn=lambda g0, glen, t2=t2:
                    xn2T[:].rearrange("p (c n) -> p c n", n=512)[:, g0:g0 + glen, t2 * 128:(t2 + 1) * 128],
                t_waits=[(ss.h, nm2)], v_waits=vwl)
            xn2_tp[(qq, t2)] = tm
        xn2_ready = sv.n

        for ot in range(8):
            for h2 in range(4):
                hidx = ot * 4 + h2
                wb = w2[hidx % 2]
                def yfn(y, qq=qq, ot=ot, h2=h2, wb=wb, hidx=hidx):
                    um = w2_used.get((qq, hidx - 2)) or (w2_used.get((qq - 1, hidx + 30)) if hidx < 2 else None)
                    if um:
                        y.wait_ge(st.h, um)
                    y.dma_start(out=wb[:], in_=s_w[ot * 128:(ot + 1) * 128,
                                                   h2 * 8 * 512:(h2 + 1) * 8 * 512]).then_inc(sy.h, 16)
                Y(yfn)
                sy.inc(16); wm = sy.n
                def tfn(t, qq=qq, ot=ot, h2=h2, wb=wb, wm=wm, hidx=hidx, xn2_ready=xn2_ready):
                    t.wait_ge(sy.h, wm)
                    if h2 == 0:
                        t.wait_ge(sv.h, xn2_ready)
                        um = ffn_evac.get((qq, ot - 1)) or (ffn_evac.get((qq - 1, 7)) if ot == 0 else None)
                        if um:
                            t.wait_ge(sv.h, um)
                    xr = xn2T[:].rearrange("p (c n) -> p c n", n=512)
                    ins = None
                    for ci in range(8):
                        c = h2 * 8 + ci
                        for t2 in range(4):
                            ins = t.matmul(ps_c[t2][:], xr[:, c, t2 * 128:(t2 + 1) * 128],
                                           wb[:, ci * 512:(ci + 1) * 512],
                                           start=(c == 0), stop=(c == 31))
                    ins.then_inc(st.h, 1)
                T(tfn)
                st.inc(); mmm = st.n
                w2_used[(qq, hidx)] = mmm
            def vfn(v, qq=qq, ot=ot, mmm=mmm):
                v.wait_ge(st.h, mmm)
                if ot == 0 and ln4_done.get(qq - 1):
                    v.wait_ge(ss.h, ln4_done[qq - 1])
                for t2 in range(4):
                    v.tensor_copy(out=fout[:, t2 * D + ot * 512:t2 * D + (ot + 1) * 512],
                                  in_=ps_c[t2][:]).then_inc(sv.h, 1)
            V(vfn)
            sv.inc(4); ffn_evac[(qq, ot)] = sv.n
        mm_last[qq] = st.n
        ffn_done = sv.n

        for t2 in range(4):
            tg = qq * 4 + t2
            xin = x1in[t2 % 2]
            gwl = list(x1_guard[t2 % 2])
            def yfn(y, tg=tg, xin=xin, gwl=gwl):
                for (h, v) in gwl:
                    y.wait_ge(h, v)
                y.dma_start(out=xin[:], in_=s_x1[tg * 128:(tg + 1) * 128, :]).then_inc(sy.h, 16)
            Y(yfn)
            sy.inc(16); xm = sy.n
            fslice = fout[:, t2 * D:(t2 + 1) * D]
            def normalize(s, scl, bias, fslice=fslice):
                s.activation(out=ln4_bf[:], in_=fslice, func=AF.Identity,
                             scale=scl, bias=bias).then_inc(ss.h, 1)
            emit_ln(fslice, normalize, D, s_waits=[(sv.h, ffn_done)], v_waits=[(sv.h, ffn_done)])
            ss.inc(); nm4 = ss.n
            def vres(v, xin=xin, nm4=nm4, xm=xm, qq=qq, t2=t2):
                v.wait_ge(ss.h, nm4)
                v.wait_ge(sy.h, xm)
                v.tensor_tensor(out=xin[:], in0=xin[:], in1=ln4_bf[:], op=ALU.add).then_inc(sv.h, 1)
            V(vres)
            sv.inc(); rm = sv.n
            def yout(y, tg=tg, rm=rm, xin=xin):
                y.wait_ge(sv.h, rm)
                y.dma_start(out=out[tg * 128:(tg + 1) * 128, :], in_=xin[:]).then_inc(so.h, 16)
            Y(yout)
            so.inc(16)
            out_w[(qq, t2)] = so.n
            x1_guard[t2 % 2] = [(so.h, so.n)]
        ln4_done[qq] = ss.n

    def yfinal(y):
        y.wait_ge(so.h, so.n)
    Y(yfinal)
    p5b.close()
    es.close()
    return nc


def _prep_inputs(L0_attn_x, L0_attn_Wq, L0_attn_bq, L0_attn_Wk, L0_attn_bk,
                 L0_attn_Wv, L0_attn_bv, L0_attn_Wo, L0_attn_bo,
                 L0_ffn_base_packed, L0_ffn_scales, L0_ffn_w_ortho):
    x = np.ascontiguousarray(np.asarray(L0_attn_x, dtype=np.float32))
    Wq = np.asarray(L0_attn_Wq, np.float32); Wk = np.asarray(L0_attn_Wk, np.float32)
    Wv = np.asarray(L0_attn_Wv, np.float32); Wo = np.asarray(L0_attn_Wo, np.float32)
    bqv = np.asarray(L0_attn_bq, np.float32); bkv = np.asarray(L0_attn_bk, np.float32)
    bvv = np.asarray(L0_attn_bv, np.float32); bov = np.asarray(L0_attn_bo, np.float32)
    bp = np.ascontiguousarray(np.asarray(L0_ffn_base_packed, np.int32))
    scv = np.asarray(L0_ffn_scales, np.float32).reshape(-1)
    wort = np.ascontiguousarray(np.asarray(L0_ffn_w_ortho, np.float32))
    zeros_bo = np.zeros((1, D), np.float32)
    in_maps = []
    for c in range(8):
        b, hh = c // 2, c % 2
        fo = slice(2048 * hh, 2048 * hh + 2048)
        m = {
            "xb": np.ascontiguousarray(x[b]),
            "x_own": np.ascontiguousarray(x[b, 2048 * hh:2048 * hh + 2048]),
            "wq": np.ascontiguousarray(Wq[fo]),
            "wk": np.ascontiguousarray(Wk[fo]),
            "wv": np.ascontiguousarray(Wv[fo]),
            "wo": np.ascontiguousarray(Wo[:, fo]),
            "bq": np.ascontiguousarray(bqv[fo].reshape(16, 128)),
            "bk": np.ascontiguousarray(bkv[fo].reshape(16, 128)),
            "bv": np.ascontiguousarray(bvv[fo].reshape(16, 128)),
            "bo": (np.ascontiguousarray(bov.reshape(1, D)) if hh == 0 else zeros_bo),
            "bp": bp,
            "sc": np.ascontiguousarray(scv.reshape(32, 128)),
            "wort": wort,
        }
        in_maps.append(m)
    return in_maps


def kernel(**inputs):
    in_maps = _prep_inputs(**inputs)
    nc = build()
    res = run_bass_kernel_spmd(nc, in_maps, core_ids=list(range(8)))
    outp = np.empty((4, 4096, 4096), np.float32)
    for c in range(8):
        b, hh = c // 2, c % 2
        outp[b, 2048 * hh:2048 * hh + 2048] = res.results[c]["out"]
    return outp


# version 06
# speedup vs baseline: 1.1936x; 1.1495x over previous
import contextlib
import math
import numpy as np
import concourse.bass as bass
import concourse.mybir as mybir
from concourse.bass_utils import run_bass_kernel_spmd

F32 = mybir.dt.float32
BF16 = mybir.dt.bfloat16
I32 = mybir.dt.int32
AF = mybir.ActivationFunctionType
ALU = mybir.AluOpType

NB, NO, FO, D = 4096, 2048, 2048, 4096
SCALE = 1.0 / math.sqrt(128.0)
EPS = 1e-5


def build(debug_stage=None):
    nc = bass.Bass("TRN2", target_bir_lowering=False)
    es = contextlib.ExitStack()

    xb = nc.dram_tensor("xb", [NB, D], F32, kind="ExternalInput").ap()
    x_own = nc.dram_tensor("x_own", [NO, D], F32, kind="ExternalInput").ap()
    wq_in = nc.dram_tensor("wq", [FO, D], F32, kind="ExternalInput").ap()
    wk_in = nc.dram_tensor("wk", [FO, D], F32, kind="ExternalInput").ap()
    wv_in = nc.dram_tensor("wv", [FO, D], F32, kind="ExternalInput").ap()
    wo_in = nc.dram_tensor("wo", [D, FO], F32, kind="ExternalInput").ap()
    bq_in = nc.dram_tensor("bq", [16, 128], F32, kind="ExternalInput").ap()
    bk_in = nc.dram_tensor("bk", [16, 128], F32, kind="ExternalInput").ap()
    bv_in = nc.dram_tensor("bv", [16, 128], F32, kind="ExternalInput").ap()
    bo_in = nc.dram_tensor("bo", [1, D], F32, kind="ExternalInput").ap()
    bp_in = nc.dram_tensor("bp", [D, D // 2], I32, kind="ExternalInput").ap()
    sc_in = nc.dram_tensor("sc", [32, 128], F32, kind="ExternalInput").ap()
    wort_in = nc.dram_tensor("wort", [D, D], F32, kind="ExternalInput").ap()
    out = nc.dram_tensor("out", [NO, D], F32, kind="ExternalOutput").ap()

    dbg = {"p2": ["s_qT", "s_kT", "s_vT"], "p3": ["s_attnT", "s_vtokd", "s_qT", "s_kT", "s_vT", "s_wqT", "s_wkT", "s_wvT"], "p5a": ["s_x1"]}.get(debug_stage, [])

    def scratch(name, shape, dtype):
        kind = "ExternalOutput" if name in dbg else "Internal"
        return nc.dram_tensor(name, shape, dtype, kind=kind).ap()

    s_wqT = scratch("s_wqT", [16 * 128, 32 * 128], BF16)
    s_wkT = scratch("s_wkT", [16 * 128, 32 * 128], BF16)
    s_wvT = scratch("s_wvT", [16 * 128, 32 * 128], BF16)
    s_woT = scratch("s_woT", [FO, D], BF16)
    s_w = scratch("s_w", [8 * 128, 32 * 512], BF16)
    s_qT = scratch("s_qT", [FO, NB], BF16)
    s_kT = scratch("s_kT", [FO, NB], BF16)
    s_vT = scratch("s_vT", [FO, NB], BF16)
    s_attnT = scratch("s_attnT", [FO, NB], BF16)
    s_vtokd = scratch("s_vtokd", [FO, NB], BF16) if debug_stage == "p3" else None
    s_part = nc.dram_tensor("s_part", [NB, D], BF16).ap()
    s_prs = nc.dram_tensor("s_prs", [NO, D], BF16).ap()
    s_x1 = scratch("s_x1", [NO, D], F32)

    blk = es.enter_context(nc.Block())

    class Sem:
        def __init__(self, name):
            self.h = es.enter_context(nc.semaphore(name))
            self.n = 0
        def inc(self, k=1):
            self.n += k
            return self.n

    sy = Sem("sy"); sa = Sem("sa"); sg = Sem("sg")
    st = Sem("st"); sv = Sem("sv"); ss = Sem("ss")
    scc = Sem("scc"); so = Sem("so")
    ALLS = [sy, sa, sg, st, sv, ss]

    def T(fn): blk.tensor(fn)
    def V(fn): blk.vector(fn)
    def S(fn): blk.scalar(fn)
    def G(fn): blk.gpsimd(fn)
    def Y(fn): blk.sync(fn)

    def fence():
        marks = [(s.h, s.n) for s in ALLS if s.n > 0]
        for E in (T, V, S, G, Y):
            def f(e, marks=marks):
                for h, v in marks:
                    e.wait_ge(h, v)
            E(f)

    ident = es.enter_context(nc.sbuf_tensor("ident", [128, 128], BF16))
    io_q = es.enter_context(nc.sbuf_tensor("io_q", [128, 512], F32))
    io_col = es.enter_context(nc.sbuf_tensor("io_col", [128, 1], F32))
    masks = es.enter_context(nc.sbuf_tensor("masks", [128, 4 * 512], BF16))
    thr = es.enter_context(nc.sbuf_tensor("thr", [128, 4], F32))
    ones_col = es.enter_context(nc.sbuf_tensor("ones_col", [128, 1], BF16))
    ones_row = es.enter_context(nc.sbuf_tensor("ones_row", [1, 128], BF16))
    bq_sb = es.enter_context(nc.sbuf_tensor("bq_sb", [128, 16], F32))
    bk_sb = es.enter_context(nc.sbuf_tensor("bk_sb", [128, 16], F32))
    bv_sb = es.enter_context(nc.sbuf_tensor("bv_sb", [128, 16], F32))
    bo_bf = es.enter_context(nc.sbuf_tensor("bo_bf", [1, D], BF16))
    sc_sb = es.enter_context(nc.sbuf_tensor("sc_sb", [128, 32], F32))
    m8s_sb = es.enter_context(nc.sbuf_tensor("m8s_sb", [128, 32], F32))
    junk1 = es.enter_context(nc.sbuf_tensor("junk1", [128, D], BF16))
    junk2 = es.enter_context(nc.sbuf_tensor("junk2", [128, D], BF16))
    st_sum = es.enter_context(nc.sbuf_tensor("st_sum", [128, 1], F32))
    st_sq = es.enter_context(nc.sbuf_tensor("st_sq", [128, 1], F32))
    st_mean = es.enter_context(nc.sbuf_tensor("st_mean", [128, 1], F32))
    st_var = es.enter_context(nc.sbuf_tensor("st_var", [128, 1], F32))
    st_std = es.enter_context(nc.sbuf_tensor("st_std", [128, 1], F32))
    st_rstd = es.enter_context(nc.sbuf_tensor("st_rstd", [128, 1], F32))
    st_nm = es.enter_context(nc.sbuf_tensor("st_nm", [128, 1], F32))

    tp_state = {"g": 0, "marks": {}, "banks": None}

    def emit_transposes(src_fn, nch, evac_dst_fn, t_waits=(), v_waits=()):
        ps_tp = tp_state["banks"]
        groups = [(g0, min(8, nch - g0)) for g0 in range(0, nch, 8)]
        for gi, (g0, glen) in enumerate(groups):
            g = tp_state["g"]
            bank = ps_tp[g % 2]
            prev = tp_state["marks"].get(g - 2)
            def tfn(t, gi=gi, g0=g0, glen=glen, bank=bank, prev=prev):
                if gi == 0:
                    for (h, v) in t_waits:
                        t.wait_ge(h, v)
                if prev is not None:
                    t.wait_ge(sv.h, prev)
                for k in range(glen):
                    ins = t.transpose(bank[:, k * 128:(k + 1) * 128], src_fn(g0 + k), ident[:])
                    if k == glen - 1:
                        ins.then_inc(st.h, 1)
            T(tfn)
            st.inc()
            tmark = st.n
            def vfn(v, gi=gi, g0=g0, glen=glen, bank=bank, tmark=tmark):
                if gi == 0:
                    for (h, vv) in v_waits:
                        v.wait_ge(h, vv)
                v.wait_ge(st.h, tmark)
                v.tensor_copy(out=evac_dst_fn(g0, glen), in_=bank[:, 0:glen * 128]).then_inc(sv.h, 1)
            V(vfn)
            sv.inc()
            tp_state["marks"][g] = sv.n
            tp_state["g"] += 1
        return st.n, sv.n

    def emit_ln(x_ap, normalize, n_free, s_waits=(), v_waits=()):
        inv_n = 1.0 / n_free
        def sfn(s):
            for (h, v) in s_waits:
                s.wait_ge(h, v)
            s.activation(out=junk1[:, 0:n_free], in_=x_ap, func=AF.Square,
                         accum_out=st_sq[:]).then_inc(ss.h, 1)
        S(sfn)
        ss.inc()
        sqm = ss.n
        sv0 = sv.n
        def vfn(v, sv0=sv0):
            for (h, vv) in v_waits:
                v.wait_ge(h, vv)
            v.tensor_scalar(out=junk2[:, 0:n_free], in0=x_ap, scalar1=1.0, scalar2=0.0,
                            op0=ALU.mult, op1=ALU.add, accum_out=st_sum[:]).then_inc(sv.h, 1)
            v.wait_ge(sv.h, sv0 + 1)
            v.wait_ge(ss.h, sqm)
            v.tensor_scalar(out=st_mean[:], in0=st_sum[:], scalar1=inv_n, scalar2=None, op0=ALU.mult)
            v.tensor_scalar(out=st_var[:], in0=st_sq[:], scalar1=inv_n, scalar2=EPS,
                            op0=ALU.mult, op1=ALU.add)
            v.scalar_tensor_tensor(out=st_nm[:], in0=st_mean[:], scalar=-1.0, in1=st_mean[:],
                                   op0=ALU.mult, op1=ALU.mult).then_inc(sv.h, 1)
            v.wait_ge(sv.h, sv0 + 2)
            v.tensor_tensor(out=st_var[:], in0=st_var[:], in1=st_nm[:], op=ALU.add).then_inc(sv.h, 1)
        V(vfn)
        sv.inc(3)
        vm1 = sv.n
        def sfn2(s):
            s.wait_ge(sv.h, vm1)
            s.activation(out=st_std[:], in_=st_var[:], func=AF.Sqrt).then_inc(ss.h, 1)
        S(sfn2)
        ss.inc()
        sq2 = ss.n
        sv1 = sv.n
        def vfn2(v, sv1=sv1):
            v.wait_ge(ss.h, sq2)
            v.reciprocal(out=st_rstd[:], in_=st_std[:]).then_inc(sv.h, 1)
            v.wait_ge(sv.h, sv1 + 1)
            v.scalar_tensor_tensor(out=st_nm[:], in0=st_mean[:], scalar=-1.0, in1=st_rstd[:],
                                   op0=ALU.mult, op1=ALU.mult).then_inc(sv.h, 1)
        V(vfn2)
        sv.inc(2)
        vm2 = sv.n
        def sfn3(s):
            s.wait_ge(sv.h, vm2)
            normalize(s, st_rstd[:], st_nm[:])
        S(sfn3)
        return vm2

    def g0(g):
        g.iota(io_q[:], [[1, 512]], base=0, channel_multiplier=0,
               allow_small_or_imprecise_dtypes=True)
        g.iota(io_col[:], [[0, 1]], base=0, channel_multiplier=1,
               allow_small_or_imprecise_dtypes=True)
        for d in range(4):
            g.iota(thr[:, d:d + 1], [[0, 1]], base=128 * d, channel_multiplier=1,
                   allow_small_or_imprecise_dtypes=True)
        g.memset(ones_col[:], 1.0)
        g.memset(ones_row[:], 1.0)
        for j in range(16):
            g.dma_start(out=bq_sb[:, j:j + 1], in_=bq_in[j].rearrange("(p one) -> p one", one=1)).then_inc(sg.h, 16)
            g.dma_start(out=bk_sb[:, j:j + 1], in_=bk_in[j].rearrange("(p one) -> p one", one=1)).then_inc(sg.h, 16)
            g.dma_start(out=bv_sb[:, j:j + 1], in_=bv_in[j].rearrange("(p one) -> p one", one=1)).then_inc(sg.h, 16)
        for j in range(32):
            g.dma_start(out=sc_sb[:, j:j + 1], in_=sc_in[j].rearrange("(p one) -> p one", one=1)).then_inc(sg.h, 16)
        g.dma_start(out=bo_bf[:], in_=bo_in).then_inc(sg.h, 16)
    G(g0)
    sg.inc(16 * 81)
    start_g = sg.n
    def v0(v):
        v.wait_ge(sg.h, start_g)
        v.tensor_scalar(out=ident[:], in0=io_q[:, 0:128], scalar1=io_col[:], scalar2=None, op0=ALU.is_equal)
        for d in range(4):
            v.tensor_scalar(out=masks[:, d * 512:(d + 1) * 512], in0=io_q[:],
                            scalar1=thr[:, d:d + 1], scalar2=None, op0=ALU.is_ge)
        v.tensor_scalar(out=m8s_sb[:], in0=sc_sb[:], scalar1=-8.0, scalar2=None, op0=ALU.mult).then_inc(sv.h, 1)
    V(v0)
    sv.inc()
    startup_done = sv.n

    p0 = contextlib.ExitStack()
    tp_state["banks"] = [p0.enter_context(nc.psum_tensor(f"ps_tp0{i}", [128, 1024], BF16)) for i in range(2)]
    w_in = [p0.enter_context(nc.sbuf_tensor(f"w_in{i}", [128, D], F32)) for i in range(2)]
    w_bf = [p0.enter_context(nc.sbuf_tensor(f"w_bf{i}", [128, D], BF16)) for i in range(2)]
    stg = [p0.enter_context(nc.sbuf_tensor(f"stg{i}", [128, D], BF16)) for i in range(2)]
    bp_b = [p0.enter_context(nc.sbuf_tensor(f"bp_b{i}", [128, D // 2], I32)) for i in range(2)]
    ort_b = [p0.enter_context(nc.sbuf_tensor(f"ort_b{i}", [128, D], F32)) for i in range(2)]
    lo_sb = p0.enter_context(nc.sbuf_tensor("lo_sb", [128, D // 2], I32))
    hi_sb = p0.enter_context(nc.sbuf_tensor("hi_sb", [128, D // 2], I32))
    dq_e = p0.enter_context(nc.sbuf_tensor("dq_e", [128, D // 2], F32))
    dq_o = p0.enter_context(nc.sbuf_tensor("dq_o", [128, D // 2], F32))

    P0G = {"win": {}, "wbf": {}, "stg": {}, "bpb": {}}

    def p0_std(w_ap, dst):
        for j in range(16):
            binp, bbf, bstg = w_in[j % 2], w_bf[j % 2], stg[j % 2]
            def yfn(y, j=j, binp=binp):
                pm = P0G["win"].get(j % 2)
                if pm:
                    y.wait_ge(pm[0], pm[1])
                y.dma_start(out=binp[:], in_=w_ap[j * 128:(j + 1) * 128, :]).then_inc(sy.h, 16)
            Y(yfn)
            sy.inc(16); ym = sy.n
            wbf_pm = P0G["wbf"].get(j % 2)
            def sfn(s, j=j, binp=binp, bbf=bbf, ym=ym, wbf_pm=wbf_pm):
                s.wait_ge(sy.h, ym)
                if wbf_pm:
                    s.wait_ge(wbf_pm[0], wbf_pm[1])
                s.activation(out=bbf[:], in_=binp[:], func=AF.Identity).then_inc(ss.h, 1)
            S(sfn)
            ss.inc(); cm = ss.n
            P0G["win"][j % 2] = (ss.h, cm)
            twl = [(ss.h, cm)] + ([(sv.h, startup_done)] if j == 0 else [])
            spm = P0G["stg"].get(j % 2)
            vwl = [spm] if spm else []
            tm, vm = emit_transposes(
                src_fn=lambda c, bbf=bbf: bbf[:, c * 128:(c + 1) * 128], nch=32,
                evac_dst_fn=lambda g0, glen, bstg=bstg: bstg[:, g0 * 128:(g0 + glen) * 128],
                t_waits=twl, v_waits=vwl)
            P0G["wbf"][j % 2] = (st.h, tm)
            def gfn(g, j=j, bstg=bstg, vm=vm, dst=dst):
                g.wait_ge(sv.h, vm)
                g.dma_start(out=dst[j * 128:(j + 1) * 128, :], in_=bstg[:]).then_inc(sg.h, 16)
            G(gfn)
            sg.inc(16)
            P0G["stg"][j % 2] = (sg.h, sg.n)

    def p0_wo():
        for j in range(16):
            binp, bbf, bstg = w_in[j % 2], w_bf[j % 2], stg[j % 2]
            def yfn(y, j=j, binp=binp):
                pm = P0G["win"].get(j % 2)
                if pm:
                    y.wait_ge(pm[0], pm[1])
                for r in range(32):
                    y.dma_start(out=binp[:, r * 128:(r + 1) * 128],
                                in_=wo_in[r * 128:(r + 1) * 128, j * 128:(j + 1) * 128]
                                ).then_inc(sy.h, 16)
            Y(yfn)
            sy.inc(16 * 32); ym = sy.n
            wbf_pm = P0G["wbf"].get(j % 2)
            def sfn(s, j=j, binp=binp, bbf=bbf, ym=ym, wbf_pm=wbf_pm):
                s.wait_ge(sy.h, ym)
                if wbf_pm:
                    s.wait_ge(wbf_pm[0], wbf_pm[1])
                s.activation(out=bbf[:], in_=binp[:], func=AF.Identity).then_inc(ss.h, 1)
            S(sfn)
            ss.inc(); cm = ss.n
            P0G["win"][j % 2] = (ss.h, cm)
            spm = P0G["stg"].get(j % 2)
            vwl = [spm] if spm else []
            tm, vm = emit_transposes(
                src_fn=lambda c, bbf=bbf: bbf[:, c * 128:(c + 1) * 128], nch=32,
                evac_dst_fn=lambda g0, glen, bstg=bstg: bstg[:, g0 * 128:(g0 + glen) * 128],
                t_waits=[(ss.h, cm)], v_waits=vwl)
            P0G["wbf"][j % 2] = (st.h, tm)
            def gfn(g, j=j, bstg=bstg, vm=vm):
                g.wait_ge(sv.h, vm)
                g.dma_start(out=s_woT[j * 128:(j + 1) * 128, :], in_=bstg[:]).then_inc(sg.h, 16)
            G(gfn)
            sg.inc(16)
            P0G["stg"][j % 2] = (sg.h, sg.n)

    def p0_ffn():
        for r in range(32):
            bbp, bor, bbf, bstg = bp_b[r % 2], ort_b[r % 2], w_bf[r % 2], stg[r % 2]
            def yfn(y, r=r, bbp=bbp, bor=bor):
                pm = P0G["bpb"].get(r % 2)
                if pm:
                    y.wait_ge(pm[0], pm[1])
                y.dma_start(out=bbp[:], in_=bp_in[r * 128:(r + 1) * 128, :]).then_inc(sy.h, 16)
                y.dma_start(out=bor[:], in_=wort_in[r * 128:(r + 1) * 128, :]).then_inc(sy.h, 16)
            Y(yfn)
            sy.inc(32); ym = sy.n
            wbf_pm = P0G["wbf"].get(r % 2)
            def vfn(v, r=r, bbp=bbp, bor=bor, bbf=bbf, ym=ym, wbf_pm=wbf_pm):
                v.wait_ge(sy.h, ym)
                if wbf_pm:
                    v.wait_ge(wbf_pm[0], wbf_pm[1])
                v.tensor_scalar(out=lo_sb[:], in0=bbp[:], scalar1=15, scalar2=None, op0=ALU.bitwise_and)
                v.tensor_scalar(out=hi_sb[:], in0=bbp[:], scalar1=4, scalar2=None, op0=ALU.arith_shift_right)
                v.tensor_scalar(out=dq_e[:], in0=lo_sb[:], scalar1=sc_sb[:, r:r + 1],
                                scalar2=m8s_sb[:, r:r + 1], op0=ALU.mult, op1=ALU.add)
                v.tensor_scalar(out=dq_o[:], in0=hi_sb[:], scalar1=sc_sb[:, r:r + 1],
                                scalar2=m8s_sb[:, r:r + 1], op0=ALU.mult, op1=ALU.add)
                v.tensor_tensor(out=bbf[:, 0:D:2], in0=dq_e[:], in1=bor[:, 0:D:2], op=ALU.add)
                v.tensor_tensor(out=bbf[:, 1:D:2], in0=dq_o[:], in1=bor[:, 1:D:2], op=ALU.add).then_inc(sv.h, 1)
            V(vfn)
            sv.inc(); cm = sv.n
            P0G["bpb"][r % 2] = (sv.h, cm)
            spm = P0G["stg"].get(r % 2)
            vwl = [spm] if spm else []
            tm, vm = emit_transposes(
                src_fn=lambda c, bbf=bbf: bbf[:, c * 128:(c + 1) * 128], nch=32,
                evac_dst_fn=lambda g0, glen, bstg=bstg: bstg[:, g0 * 128:(g0 + glen) * 128],
                t_waits=[(sv.h, cm)], v_waits=vwl)
            P0G["wbf"][r % 2] = (st.h, tm)
            ot, rr = r // 4, r % 4
            def gfn(g, r=r, bstg=bstg, vm=vm, ot=ot, rr=rr):
                g.wait_ge(sv.h, vm)
                dstblk = s_w[ot * 128:(ot + 1) * 128, :].rearrange("p (c o) -> p c o", o=512)
                dst = dstblk[:, :, rr * 128:(rr + 1) * 128]
                srcb = bstg[:].rearrange("p (c o) -> p c o", o=128)
                for h4 in range(4):
                    g.dma_start(out=dst[:, h4 * 8:(h4 + 1) * 8, :],
                                in_=srcb[:, h4 * 8:(h4 + 1) * 8, :],
                                allow_slow_non_contiguous=True).then_inc(sg.h, 16)
            G(gfn)
            sg.inc(64)
            P0G["stg"][r % 2] = (sg.h, sg.n)

    p0_std(wq_in, s_wqT)
    p0_std(wk_in, s_wkT)
    p0_std(wv_in, s_wvT)
    p0_wo()
    p0_ffn()
    p0.close()
    fence()

    p2 = contextlib.ExitStack()
    tp_state["banks"] = [p2.enter_context(nc.psum_tensor(f"ps_tp2{i}", [128, 1024], BF16)) for i in range(2)]
    x_b = [p2.enter_context(nc.sbuf_tensor(f"x_b{i}", [128, D], F32)) for i in range(2)]
    xn_tok = [p2.enter_context(nc.sbuf_tensor(f"xn_tok{i}", [128, D], BF16)) for i in range(2)]
    xnT = [p2.enter_context(nc.sbuf_tensor(f"xnT{i}", [128, 32 * 512], BF16)) for i in range(2)]
    wblk = [p2.enter_context(nc.sbuf_tensor(f"wblk{i}", [128, D], BF16)) for i in range(2)]
    pstg = [p2.enter_context(nc.sbuf_tensor(f"pstg{i}", [128, 512], BF16)) for i in range(2)]
    ps_mm = [p2.enter_context(nc.psum_tensor(f"ps_mm{i}", [128, 512], F32)) for i in range(2)]

    WLIST = [("q", s_wqT, s_qT, bq_sb), ("k", s_wkT, s_kT, bk_sb), ("v", s_wvT, s_vT, bv_sb)]
    xn_used = {}
    xn_tok_tp = {}
    wblk_used = {}
    evac_marks = {}
    pstg_w = {}
    xload = {}
    widx = 0

    for e in range(8):
        nbuf = xnT[e % 2]
        for tci in range(4):
            tc = e * 4 + tci
            xbuf = x_b[tci % 2]
            ntk = xn_tok[tci % 2]
            gtc = e * 4 + tci
            def yfn(y, e=e, tci=tci, tc=tc, xbuf=xbuf):
                pm = xload.get((e, tci - 2)) or (xload.get((e - 1, tci + 2)) if tci < 2 else None)
                if pm:
                    y.wait_ge(ss.h, pm)
                y.dma_start(out=xbuf[:], in_=xb[tc * 128:(tc + 1) * 128, :]).then_inc(sy.h, 16)
            Y(yfn)
            sy.inc(16); ym = sy.n
            ntk_free = xn_tok_tp.get(gtc - 2)
            def normalize(s, scl, bias, xbuf=xbuf, ntk=ntk, ntk_free=ntk_free):
                if ntk_free:
                    s.wait_ge(st.h, ntk_free)
                s.activation(out=ntk[:], in_=xbuf[:], func=AF.Identity,
                             scale=scl, bias=bias).then_inc(ss.h, 1)
            emit_ln(xbuf[:], normalize, D, s_waits=[(sy.h, ym)], v_waits=[(sy.h, ym)])
            ss.inc()
            xload[(e, tci)] = ss.n
            nm = ss.n
            vwl = [(st.h, xn_used[e - 2])] if xn_used.get(e - 2) else []
            tm, _ = emit_transposes(
                src_fn=lambda c, ntk=ntk: ntk[:, c * 128:(c + 1) * 128], nch=32,
                evac_dst_fn=lambda g0, glen, nbuf=nbuf, tci=tci:
                    nbuf[:].rearrange("p (c n) -> p c n", n=512)[:, g0:g0 + glen, tci * 128:(tci + 1) * 128],
                t_waits=[(ss.h, nm)], v_waits=vwl)
            xn_tok_tp[gtc] = tm
        xnT_ready = sv.n

        for (wname, wscr, dscr, bias_sb) in WLIST:
            for j in range(16):
                wb = wblk[widx % 2]
                def yfn(y, wb=wb, wscr=wscr, j=j, widx=widx):
                    um = wblk_used.get(widx - 2)
                    if um:
                        y.wait_ge(st.h, um)
                    y.dma_start(out=wb[:], in_=wscr[j * 128:(j + 1) * 128, :]).then_inc(sy.h, 16)
                Y(yfn)
                sy.inc(16); wm = sy.n
                bank = ps_mm[widx % 2]
                def tfn(t, wb=wb, bank=bank, wm=wm, widx=widx, nbuf=nbuf, xnT_ready=xnT_ready):
                    t.wait_ge(sy.h, wm)
                    t.wait_ge(sv.h, xnT_ready)
                    um = evac_marks.get(widx - 2)
                    if um:
                        t.wait_ge(ss.h, um)
                    rhs = nbuf[:].rearrange("p (c n) -> p c n", n=512)
                    ins = None
                    for c in range(32):
                        ins = t.matmul(bank[:], wb[:, c * 128:(c + 1) * 128], rhs[:, c, :],
                                       start=(c == 0), stop=(c == 31))
                    ins.then_inc(st.h, 1)
                T(tfn)
                st.inc(); mm = st.n
                wblk_used[widx] = mm
                pbuf = pstg[widx % 2]
                def sfn(s, bank=bank, pbuf=pbuf, mm=mm, bias_sb=bias_sb, j=j, widx=widx):
                    s.wait_ge(st.h, mm)
                    um = pstg_w.get(widx - 2)
                    if um:
                        s.wait_ge(sa.h, um)
                    s.activation(out=pbuf[:], in_=bank[:], func=AF.Identity,
                                 bias=bias_sb[:, j:j + 1], scale=1.0).then_inc(ss.h, 1)
                S(sfn)
                ss.inc(); em = ss.n
                evac_marks[widx] = em
                def afn(a, pbuf=pbuf, dscr=dscr, j=j, e=e, em=em):
                    a.wait_ge(ss.h, em)
                    a.dma_start(out=dscr[j * 128:(j + 1) * 128, e * 512:(e + 1) * 512],
                                in_=pbuf[:]).then_inc(sa.h, 16)
                S(afn)
                sa.inc(16)
                pstg_w[widx] = sa.n
                widx += 1
        xn_used[e] = st.n

    p2.close()
    fence()
    if debug_stage == "p2":
        es.close()
        return nc

    p3 = contextlib.ExitStack()
    qh = [p3.enter_context(nc.sbuf_tensor(f"qh{i}", [128, NB], BF16)) for i in range(2)]
    kh = [p3.enter_context(nc.sbuf_tensor(f"kh{i}", [128, NB], BF16)) for i in range(2)]
    vh = [p3.enter_context(nc.sbuf_tensor(f"vh{i}", [128, NB], BF16)) for i in range(2)]
    vtok = p3.enter_context(nc.sbuf_tensor("vtok", [128, NB], BF16))
    p_sb = [p3.enter_context(nc.sbuf_tensor(f"p_sb{i}", [128, 4 * 512], BF16)) for i in range(2)]
    rb_f = p3.enter_context(nc.sbuf_tensor("rb_f", [1, 512], F32))
    rb_bf = p3.enter_context(nc.sbuf_tensor("rb_bf", [1, 512], BF16))
    rbc = p3.enter_context(nc.sbuf_tensor("rbc", [128, 512], F32))
    astg = [p3.enter_context(nc.sbuf_tensor(f"astg{i}", [128, 512], BF16)) for i in range(2)]
    ps_s = [p3.enter_context(nc.psum_tensor(f"ps_s{i}", [128, 512], F32)) for i in range(4)]
    ps_o = p3.enter_context(nc.psum_tensor("ps_o", [128, 512], F32))
    ps_u = p3.enter_context(nc.psum_tensor("ps_u", [128, 512], F32))
    ps_tp3 = [p3.enter_context(nc.psum_tensor(f"ps_tp3{i}", [128, 1024], BF16)) for i in range(2)]

    head_done = {}
    sbm = {}
    uom = {}
    expb = {}
    maskb = {}
    v1m = {}
    r1m = {}
    rcopy = {}
    evacA = {}
    astg_w = {}
    gb = [0]
    gmap = {}

    def e_Sb(hl, qt, b, extra=()):
        g = gb[0]
        prevk = gmap.get(g - 1)
        def f(t, hl=hl, qt=qt, b=b, prevk=prevk, extra=tuple(extra)):
            for (h, v) in extra:
                t.wait_ge(h, v)
            if prevk:
                t.wait_ge(ss.h, expb[prevk])
            for i4 in range(4):
                kc = 4 * b + i4
                ins = t.matmul(ps_s[i4][:], kh[hl % 2][:, kc * 128:(kc + 1) * 128],
                               qh[hl % 2][:, qt * 512:(qt + 1) * 512],
                               start=True, stop=True)
            ins.then_inc(st.h, 1)
        T(f)
        st.inc()
        sbm[(hl, qt, b)] = st.n
        gmap[g] = (hl, qt, b)
        gb[0] += 1
        return g

    def e_EXPb(hl, qt, b, g):
        pset = g % 2
        pb = gmap.get(g - 2)
        def f(s, hl=hl, qt=qt, b=b, pset=pset, pb=pb):
            s.wait_ge(st.h, sbm[(hl, qt, b)])
            if pb:
                s.wait_ge(st.h, uom[pb])
            for i4 in range(4):
                s.activation(out=p_sb[pset][:, i4 * 512:(i4 + 1) * 512], in_=ps_s[i4][:],
                             func=AF.Exp, scale=SCALE).then_inc(ss.h, 1)
        S(f)
        ss.inc(4)
        expb[(hl, qt, b)] = ss.n

    def e_MASKb(hl, qt, b, g):
        pset = g % 2
        def f(v, hl=hl, qt=qt, b=b, pset=pset):
            v.wait_ge(ss.h, expb[(hl, qt, b)])
            for i4 in range(4):
                sl = slice(i4 * 512, (i4 + 1) * 512)
                v.tensor_tensor(out=p_sb[pset][:, sl], in0=p_sb[pset][:, sl],
                                in1=masks[:, i4 * 512:(i4 + 1) * 512], op=ALU.mult).then_inc(sv.h, 1)
        V(f)
        sv.inc(4)
        maskb[(hl, qt)] = sv.n

    def e_UOb(hl, qt, b, g, K, vt_ready):
        pset = g % 2
        diag = (4 * b + 3) >= 4 * qt
        first, last = b == 0, 4 * b + 4 >= K
        prev = (hl, qt - 1) if qt >= 1 else ((hl - 1, 7) if hl >= 1 else None)
        def f(t, hl=hl, qt=qt, b=b, pset=pset, diag=diag, first=first, last=last,
              prev=prev, vt_ready=vt_ready, K=K):
            if diag:
                t.wait_ge(sv.h, maskb[(hl, qt)])
            else:
                t.wait_ge(ss.h, expb[(hl, qt, b)])
            if first:
                t.wait_ge(sv.h, vt_ready)
                if prev:
                    t.wait_ge(sv.h, rcopy[prev])
                    t.wait_ge(sv.h, evacA[prev])
            for i4 in range(4):
                kc = 4 * b + i4
                psl = p_sb[pset][:, i4 * 512:(i4 + 1) * 512]
                fi = first and i4 == 0
                la = last and i4 == 3
                t.matmul(ps_u[0:1, :], ones_col[:], psl, start=fi, stop=la)
                ins = t.matmul(ps_o[:], vtok[:, kc * 128:(kc + 1) * 128], psl,
                               start=fi, stop=la)
            ins.then_inc(st.h, 1)
        T(f)
        st.inc()
        uom[(hl, qt, b)] = st.n

    def e_V1(hl, qt, K):
        prev_rb = (hl, qt - 1) if qt >= 1 else ((hl - 1, 7) if hl >= 1 else None)
        def f(v, hl=hl, qt=qt, K=K, prev_rb=prev_rb):
            v.wait_ge(st.h, uom[(hl, qt, (K - 4) // 4)])
            if prev_rb and prev_rb in r1m:
                v.wait_ge(st.h, r1m[prev_rb])
            v.tensor_copy(out=rb_f[:], in_=ps_u[0:1, :])
            v.reciprocal(out=rb_f[:], in_=rb_f[:])
            v.tensor_copy(out=rb_bf[:], in_=rb_f[:]).then_inc(sv.h, 1)
        V(f)
        sv.inc()
        v1m[(hl, qt)] = sv.n

    def e_R(hl, qt):
        def f(t, hl=hl, qt=qt):
            t.wait_ge(sv.h, v1m[(hl, qt)])
            t.matmul(ps_u[:], ones_row[:], rb_bf[:], start=True, stop=True).then_inc(st.h, 1)
        T(f)
        st.inc()
        r1m[(hl, qt)] = st.n

    def e_V2(hl, qt):
        ab = astg[qt % 2]
        um = astg_w.get((hl, qt - 2)) or (astg_w.get((hl - 1, qt + 6)) if qt < 2 else None)
        def f(v, hl=hl, qt=qt, ab=ab, um=um):
            v.wait_ge(st.h, r1m[(hl, qt)])
            if um:
                v.wait_ge(sa.h, um)
            v.tensor_copy(out=rbc[:], in_=ps_u[:]).then_inc(sv.h, 1)
            v.tensor_tensor(out=ab[:], in0=ps_o[:], in1=rbc[:], op=ALU.mult).then_inc(sv.h, 1)
        V(f)
        sv.inc(); rcopy[(hl, qt)] = sv.n
        sv.inc(); evacA[(hl, qt)] = sv.n

    def e_AST(hl, qt):
        ab = astg[qt % 2]
        am = evacA[(hl, qt)]
        def f(a, hl=hl, qt=qt, ab=ab, am=am):
            a.wait_ge(sv.h, am)
            a.dma_start(out=s_attnT[hl * 128:(hl + 1) * 128, qt * 512:(qt + 1) * 512],
                        in_=ab[:]).then_inc(sa.h, 16)
        S(f)
        sa.inc(16)
        astg_w[(hl, qt)] = sa.n

    def flush(hl, qt):
        e_R(hl, qt); e_V2(hl, qt); e_AST(hl, qt)

    for hl in range(16):
        qb, kb, vb = qh[hl % 2], kh[hl % 2], vh[hl % 2]
        def afn(a, hl=hl, qb=qb, kb=kb, vb=vb):
            um = head_done.get(hl - 2)
            if um:
                a.wait_ge(st.h, um)
            a.dma_start(out=qb[:], in_=s_qT[hl * 128:(hl + 1) * 128, :]).then_inc(sa.h, 16)
            a.dma_start(out=kb[:], in_=s_kT[hl * 128:(hl + 1) * 128, :]).then_inc(sa.h, 16)
            a.dma_start(out=vb[:], in_=s_vT[hl * 128:(hl + 1) * 128, :]).then_inc(sa.h, 16)
        S(afn)
        sa.inc(48); hm = sa.n
        if hl >= 1:
            flush(hl - 1, 7)
        twl = [(sa.h, hm)]
        vwl = [(st.h, head_done[hl - 1])] if head_done.get(hl - 1) else []
        tp_state["banks"] = ps_tp3
        _, vt_ready = emit_transposes(
            src_fn=lambda c, vb=vb: vb[:, c * 128:(c + 1) * 128], nch=32,
            evac_dst_fn=lambda g0, glen: vtok[:, g0 * 128:(g0 + glen) * 128],
            t_waits=twl, v_waits=vwl)

        for qt in range(8):
            K = 4 * qt + 4
            nb = qt + 1
            g0 = e_Sb(hl, qt, 0, extra=[(sa.h, hm)] if qt == 0 else ())
            e_EXPb(hl, qt, 0, g0)
            if 0 == qt // 1 and 3 >= 4 * qt:
                pass
            if qt == 0:
                e_MASKb(hl, qt, 0, g0)
            if qt >= 1:
                flush(hl, qt - 1)
            gprev = g0
            for b in range(nb):
                if b + 1 < nb:
                    gn = e_Sb(hl, qt, b + 1)
                    e_EXPb(hl, qt, b + 1, gn)
                    if b + 1 == nb - 1:
                        e_MASKb(hl, qt, b + 1, gn)
                else:
                    gn = None
                e_UOb(hl, qt, b, gprev, K, vt_ready)
                gprev = gn
            e_V1(hl, qt, K)
        head_done[hl] = st.n

    flush(15, 7)
    p3.close()
    fence()
    if debug_stage == "p3":
        es.close()
        return nc

    p4 = contextlib.ExitStack()
    woT_all = p4.enter_context(nc.sbuf_tensor("woT_all", [128, 16 * D], BF16))
    aT = [p4.enter_context(nc.sbuf_tensor(f"aT{i}", [128, FO], BF16)) for i in range(2)]
    prow = [p4.enter_context(nc.sbuf_tensor(f"prow{i}", [128, D], BF16)) for i in range(2)]
    ps_b = [p4.enter_context(nc.psum_tensor(f"ps_b{i}", [128, 512], F32)) for i in range(2)]

    def yw(y):
        for j in range(16):
            y.dma_start(out=woT_all[:, j * D:(j + 1) * D],
                        in_=s_woT[j * 128:(j + 1) * 128, :]).then_inc(sy.h, 16)
    Y(yw)
    sy.inc(16 * 16); wo_ld = sy.n

    aT_ld = {}
    mm_done = {}
    evacP = {}
    prow_w = {}
    for m in range(32):
        ab = aT[m % 2]
        def afn(a, m=m, ab=ab):
            um = mm_done.get(m - 2)
            if um:
                a.wait_ge(st.h, um)
            for j in range(16):
                a.dma_start(out=ab[:, j * 128:(j + 1) * 128],
                            in_=s_attnT[j * 128:(j + 1) * 128, m * 128:(m + 1) * 128]).then_inc(sa.h, 16)
        S(afn)
        sa.inc(16 * 16); am = sa.n
        aT_ld[m] = am
        pr = prow[m % 2]
        for ot in range(8):
            bank = ps_b[ot % 2]
            def tfn(t, m=m, ot=ot, bank=bank, ab=ab, am=am):
                if m == 0 and ot == 0:
                    t.wait_ge(sy.h, wo_ld)
                t.wait_ge(sa.h, am)
                um = evacP.get((m, ot - 2)) or (evacP.get((m - 1, ot + 6)) if ot < 2 else None)
                if um:
                    t.wait_ge(sv.h, um)
                for j in range(16):
                    t.matmul(bank[:], ab[:, j * 128:(j + 1) * 128],
                             woT_all[:, j * D + ot * 512:j * D + (ot + 1) * 512],
                             start=(j == 0), stop=False)
                t.matmul(bank[:], ones_row[:], bo_bf[0:1, ot * 512:(ot + 1) * 512],
                         start=False, stop=True).then_inc(st.h, 1)
            T(tfn)
            st.inc(); mmm = st.n
            def vfn(v, m=m, ot=ot, bank=bank, pr=pr, mmm=mmm):
                v.wait_ge(st.h, mmm)
                if ot == 0:
                    um = prow_w.get(m - 2)
                    if um:
                        v.wait_ge(sg.h, um)
                v.tensor_copy(out=pr[:, ot * 512:(ot + 1) * 512], in_=bank[:]).then_inc(sv.h, 1)
            V(vfn)
            sv.inc(); evacP[(m, ot)] = sv.n
        mm_done[m] = st.n
        vm = sv.n
        def gfn(g, m=m, pr=pr, vm=vm):
            g.wait_ge(sv.h, vm)
            g.dma_start(out=s_part[m * 128:(m + 1) * 128, :], in_=pr[:]).then_inc(sg.h, 16)
        G(gfn)
        sg.inc(16); prow_w[m] = sg.n

    part_done = sg.n
    def gcc(g):
        g.wait_ge(sg.h, part_done)
        g.collective_compute(
            "ReduceScatter", ALU.add,
            replica_groups=[[0, 1], [2, 3], [4, 5], [6, 7]],
            ins=[s_part.opt()],
            outs=[s_prs.opt()],
        ).then_inc(scc.h, 1)
    G(gcc)
    scc.inc()
    p4.close()
    fence()

    p5a = contextlib.ExitStack()
    pr_b = [p5a.enter_context(nc.sbuf_tensor(f"pr_b{i}", [128, D], BF16)) for i in range(2)]
    xo_b = [p5a.enter_context(nc.sbuf_tensor(f"xo_b{i}", [128, D], F32)) for i in range(2)]
    ln2_bf = p5a.enter_context(nc.sbuf_tensor("ln2_bf", [128, D], BF16))
    x1_b = [p5a.enter_context(nc.sbuf_tensor(f"x1_b{i}", [128, D], F32)) for i in range(2)]

    norm_done = {}
    x1_w = {}
    res_done = {}
    for t_ in range(16):
        prb, xob, x1b = pr_b[t_ % 2], xo_b[t_ % 2], x1_b[t_ % 2]
        def afn(a, t_=t_, prb=prb):
            if t_ == 0:
                a.wait_ge(scc.h, 1)
            um = norm_done.get(t_ - 2)
            if um:
                a.wait_ge(ss.h, um)
            a.dma_start(out=prb[:], in_=s_prs[t_ * 128:(t_ + 1) * 128, :]).then_inc(sa.h, 16)
        S(afn)
        sa.inc(16); pm = sa.n
        def yfn(y, t_=t_, xob=xob):
            um = res_done.get(t_ - 2)
            if um:
                y.wait_ge(sv.h, um)
            y.dma_start(out=xob[:], in_=x_own[t_ * 128:(t_ + 1) * 128, :]).then_inc(sy.h, 16)
        Y(yfn)
        sy.inc(16); xm = sy.n
        def normalize(s, scl, bias, prb=prb):
            s.activation(out=ln2_bf[:], in_=prb[:], func=AF.Identity,
                         scale=scl, bias=bias).then_inc(ss.h, 1)
        emit_ln(prb[:], normalize, D, s_waits=[(sa.h, pm)], v_waits=[(sa.h, pm)])
        ss.inc(); norm_done[t_] = ss.n; nmk = ss.n
        def vres(v, t_=t_, xob=xob, x1b=x1b, nmk=nmk, xm=xm):
            v.wait_ge(ss.h, nmk)
            v.wait_ge(sy.h, xm)
            um = x1_w.get(t_ - 2)
            if um:
                v.wait_ge(sg.h, um)
            v.tensor_tensor(out=x1b[:], in0=xob[:], in1=ln2_bf[:], op=ALU.add).then_inc(sv.h, 1)
        V(vres)
        sv.inc(); res_done[t_] = sv.n; rm = sv.n
        def gfn(g, t_=t_, x1b=x1b, rm=rm):
            g.wait_ge(sv.h, rm)
            g.dma_start(out=s_x1[t_ * 128:(t_ + 1) * 128, :], in_=x1b[:]).then_inc(sg.h, 16)
        G(gfn)
        sg.inc(16); x1_w[t_] = sg.n

    p5a.close()
    fence()
    if debug_stage == "p5a":
        es.close()
        return nc

    p5b = contextlib.ExitStack()
    tp_state["banks"] = [p5b.enter_context(nc.psum_tensor(f"ps_tp5{i}", [128, 1024], BF16)) for i in range(2)]
    x1in = [p5b.enter_context(nc.sbuf_tensor(f"x1in{i}", [128, D], F32)) for i in range(2)]
    xn2_tok = [p5b.enter_context(nc.sbuf_tensor(f"xn2_tok{i}", [128, D], BF16)) for i in range(2)]
    xn2T = p5b.enter_context(nc.sbuf_tensor("xn2T", [128, 32 * 512], BF16))
    w2 = [p5b.enter_context(nc.sbuf_tensor(f"w2{i}", [128, 8 * 512], BF16)) for i in range(2)]
    fout = p5b.enter_context(nc.sbuf_tensor("fout", [128, 4 * D], BF16))
    ln4_bf = p5b.enter_context(nc.sbuf_tensor("ln4_bf", [128, D], BF16))
    ps_c = [p5b.enter_context(nc.psum_tensor(f"ps_c{i}", [128, 512], F32)) for i in range(4)]

    w2_used = {}
    ffn_evac = {}
    ln3_done = {}
    ln4_done = {}
    x1_guard = {0: [], 1: []}
    xn2_tp = {}
    out_w = {}
    mm_last = {}
    for qq in range(4):
        for t2 in range(4):
            tg = qq * 4 + t2
            xin = x1in[t2 % 2]
            ntk = xn2_tok[t2 % 2]
            gwl = list(x1_guard[t2 % 2])
            def yfn(y, tg=tg, xin=xin, gwl=gwl):
                for (h, v) in gwl:
                    y.wait_ge(h, v)
                y.dma_start(out=xin[:], in_=s_x1[tg * 128:(tg + 1) * 128, :]).then_inc(sy.h, 16)
            Y(yfn)
            sy.inc(16); xm = sy.n
            ntk_free = xn2_tp.get((qq, t2 - 2)) or (xn2_tp.get((qq - 1, t2 + 2)) if t2 < 2 else None)
            def normalize(s, scl, bias, xin=xin, ntk=ntk, ntk_free=ntk_free):
                if ntk_free:
                    s.wait_ge(st.h, ntk_free)
                s.activation(out=ntk[:], in_=xin[:], func=AF.Identity,
                             scale=scl, bias=bias).then_inc(ss.h, 1)
            emit_ln(xin[:], normalize, D, s_waits=[(sy.h, xm)], v_waits=[(sy.h, xm)])
            ss.inc(); ln3_done[(qq, t2)] = ss.n; nm2 = ss.n
            x1_guard[t2 % 2] = [(ss.h, ss.n)]
            vwl = [(st.h, mm_last[qq - 1])] if mm_last.get(qq - 1) else []
            tm, _ = emit_transposes(
                src_fn=lambda c, ntk=ntk: ntk[:, c * 128:(c + 1) * 128], nch=32,
                evac_dst_fn=lambda g0, glen, t2=t2:
                    xn2T[:].rearrange("p (c n) -> p c n", n=512)[:, g0:g0 + glen, t2 * 128:(t2 + 1) * 128],
                t_waits=[(ss.h, nm2)], v_waits=vwl)
            xn2_tp[(qq, t2)] = tm
        xn2_ready = sv.n

        for ot in range(8):
            for h2 in range(4):
                hidx = ot * 4 + h2
                wb = w2[hidx % 2]
                def yfn(y, qq=qq, ot=ot, h2=h2, wb=wb, hidx=hidx):
                    um = w2_used.get((qq, hidx - 2)) or (w2_used.get((qq - 1, hidx + 30)) if hidx < 2 else None)
                    if um:
                        y.wait_ge(st.h, um)
                    y.dma_start(out=wb[:], in_=s_w[ot * 128:(ot + 1) * 128,
                                                   h2 * 8 * 512:(h2 + 1) * 8 * 512]).then_inc(sy.h, 16)
                Y(yfn)
                sy.inc(16); wm = sy.n
                def tfn(t, qq=qq, ot=ot, h2=h2, wb=wb, wm=wm, hidx=hidx, xn2_ready=xn2_ready):
                    t.wait_ge(sy.h, wm)
                    if h2 == 0:
                        t.wait_ge(sv.h, xn2_ready)
                        um = ffn_evac.get((qq, ot - 1)) or (ffn_evac.get((qq - 1, 7)) if ot == 0 else None)
                        if um:
                            t.wait_ge(sv.h, um)
                    xr = xn2T[:].rearrange("p (c n) -> p c n", n=512)
                    ins = None
                    for ci in range(8):
                        c = h2 * 8 + ci
                        for t2 in range(4):
                            ins = t.matmul(ps_c[t2][:], xr[:, c, t2 * 128:(t2 + 1) * 128],
                                           wb[:, ci * 512:(ci + 1) * 512],
                                           start=(c == 0), stop=(c == 31))
                    ins.then_inc(st.h, 1)
                T(tfn)
                st.inc(); mmm = st.n
                w2_used[(qq, hidx)] = mmm
            def vfn(v, qq=qq, ot=ot, mmm=mmm):
                v.wait_ge(st.h, mmm)
                if ot == 0 and ln4_done.get(qq - 1):
                    v.wait_ge(ss.h, ln4_done[qq - 1])
                for t2 in range(4):
                    v.tensor_copy(out=fout[:, t2 * D + ot * 512:t2 * D + (ot + 1) * 512],
                                  in_=ps_c[t2][:]).then_inc(sv.h, 1)
            V(vfn)
            sv.inc(4); ffn_evac[(qq, ot)] = sv.n
        mm_last[qq] = st.n
        ffn_done = sv.n

        for t2 in range(4):
            tg = qq * 4 + t2
            xin = x1in[t2 % 2]
            gwl = list(x1_guard[t2 % 2])
            def yfn(y, tg=tg, xin=xin, gwl=gwl):
                for (h, v) in gwl:
                    y.wait_ge(h, v)
                y.dma_start(out=xin[:], in_=s_x1[tg * 128:(tg + 1) * 128, :]).then_inc(sy.h, 16)
            Y(yfn)
            sy.inc(16); xm = sy.n
            fslice = fout[:, t2 * D:(t2 + 1) * D]
            def normalize(s, scl, bias, fslice=fslice):
                s.activation(out=ln4_bf[:], in_=fslice, func=AF.Identity,
                             scale=scl, bias=bias).then_inc(ss.h, 1)
            emit_ln(fslice, normalize, D, s_waits=[(sv.h, ffn_done)], v_waits=[(sv.h, ffn_done)])
            ss.inc(); nm4 = ss.n
            def vres(v, xin=xin, nm4=nm4, xm=xm, qq=qq, t2=t2):
                v.wait_ge(ss.h, nm4)
                v.wait_ge(sy.h, xm)
                v.tensor_tensor(out=xin[:], in0=xin[:], in1=ln4_bf[:], op=ALU.add).then_inc(sv.h, 1)
            V(vres)
            sv.inc(); rm = sv.n
            def yout(y, tg=tg, rm=rm, xin=xin):
                y.wait_ge(sv.h, rm)
                y.dma_start(out=out[tg * 128:(tg + 1) * 128, :], in_=xin[:]).then_inc(so.h, 16)
            Y(yout)
            so.inc(16)
            out_w[(qq, t2)] = so.n
            x1_guard[t2 % 2] = [(so.h, so.n)]
        ln4_done[qq] = ss.n

    def yfinal(y):
        y.wait_ge(so.h, so.n)
    Y(yfinal)
    p5b.close()
    es.close()
    return nc


def _prep_inputs(L0_attn_x, L0_attn_Wq, L0_attn_bq, L0_attn_Wk, L0_attn_bk,
                 L0_attn_Wv, L0_attn_bv, L0_attn_Wo, L0_attn_bo,
                 L0_ffn_base_packed, L0_ffn_scales, L0_ffn_w_ortho):
    x = np.ascontiguousarray(np.asarray(L0_attn_x, dtype=np.float32))
    Wq = np.asarray(L0_attn_Wq, np.float32); Wk = np.asarray(L0_attn_Wk, np.float32)
    Wv = np.asarray(L0_attn_Wv, np.float32); Wo = np.asarray(L0_attn_Wo, np.float32)
    bqv = np.asarray(L0_attn_bq, np.float32); bkv = np.asarray(L0_attn_bk, np.float32)
    bvv = np.asarray(L0_attn_bv, np.float32); bov = np.asarray(L0_attn_bo, np.float32)
    bp = np.ascontiguousarray(np.asarray(L0_ffn_base_packed, np.int32))
    scv = np.asarray(L0_ffn_scales, np.float32).reshape(-1)
    wort = np.ascontiguousarray(np.asarray(L0_ffn_w_ortho, np.float32))
    zeros_bo = np.zeros((1, D), np.float32)
    in_maps = []
    for c in range(8):
        b, hh = c // 2, c % 2
        fo = slice(2048 * hh, 2048 * hh + 2048)
        m = {
            "xb": np.ascontiguousarray(x[b]),
            "x_own": np.ascontiguousarray(x[b, 2048 * hh:2048 * hh + 2048]),
            "wq": np.ascontiguousarray(Wq[fo]),
            "wk": np.ascontiguousarray(Wk[fo]),
            "wv": np.ascontiguousarray(Wv[fo]),
            "wo": np.ascontiguousarray(Wo[:, fo]),
            "bq": np.ascontiguousarray(bqv[fo].reshape(16, 128)),
            "bk": np.ascontiguousarray(bkv[fo].reshape(16, 128)),
            "bv": np.ascontiguousarray(bvv[fo].reshape(16, 128)),
            "bo": (np.ascontiguousarray(bov.reshape(1, D)) if hh == 0 else zeros_bo),
            "bp": bp,
            "sc": np.ascontiguousarray(scv.reshape(32, 128)),
            "wort": wort,
        }
        in_maps.append(m)
    return in_maps


def kernel(**inputs):
    in_maps = _prep_inputs(**inputs)
    nc = build()
    res = run_bass_kernel_spmd(nc, in_maps, core_ids=list(range(8)))
    outp = np.empty((4, 4096, 4096), np.float32)
    for c in range(8):
        b, hh = c // 2, c % 2
        outp[b, 2048 * hh:2048 * hh + 2048] = res.results[c]["out"]
    return outp
